# version 15
# speedup vs baseline: 1.0841x; 1.0020x over previous
import numpy as np
import ml_dtypes
from contextlib import ExitStack, nullcontext

import concourse.bass as bass
import concourse.mybir as mybir
from concourse.bass_utils import run_bass_kernel_spmd

F32 = mybir.dt.float32
BF16 = mybir.dt.bfloat16
AF = mybir.ActivationFunctionType
ALU = mybir.AluOpType
AX = mybir.AxisListType.X
BF = ml_dtypes.bfloat16

NC = 8
B, T, D = 4, 4096, 1024
CTOK = 2048
NH, DH = 16, 64
EPS = 1e-5
NEG = -1.0e30

DBG_MPOST = False
FUSED = True
NKT_PAD = [12, 16, 28, 32]
DIAG_SLOTS = [[0, 1, 2, 3, 8, 9, 10, 11],
              [4, 5, 6, 7, 12, 13, 14, 15],
              [16, 17, 18, 19, 24, 25, 26, 27],
              [20, 21, 22, 23, 28, 29, 30, 31]]

ENGS = ("sync", "tensor", "vector", "scalar", "gpsimd")


def core_quarters(c):
    b = c // 2
    if c % 2 == 0:
        return b, [(0, 1024), (3072, 4096)]
    return b, [(1024, 2048), (2048, 3072)]


def core_qstarts(c):
    _, qr = core_quarters(c)
    return [qr[0][0], qr[0][0] + 512, qr[1][0], qr[1][0] + 512]


class Prog:

    def __init__(self, nc):
        self.nc = nc
        self.q = {e: [] for e in ENGS}
        self.c = {e: 0 for e in ENGS}
        self._fence = None
        self._fenced = set()

    def op(self, eng, build, deps=(), dma=False):
        w = []
        if self._fence is not None and eng not in self._fenced:
            for s, v in self._fence.items():
                if s != eng and v > 0:
                    w.append((s, v))
            self._fenced.add(eng)
            if len(self._fenced) == len(ENGS):
                self._fence = None
        for d in deps:
            if isinstance(d, tuple):
                s, v = d
                if v > 0:
                    w.append((s, v))
            else:
                s, v = d, self.c[d]
                if s != eng and v > 0:
                    w.append((s, v))
        step = 16 if (dma or eng == "sync") else 1
        self.c[eng] += step
        self.q[eng].append((build, w, step))
        return self.c[eng]

    def fence(self):
        self._fence = dict(self.c)
        self._fenced = set()

    def sop(self, eng, build, deps=(), dma=False):
        return self.op(eng, build, deps=list(deps) + [(eng, self.c[eng])], dma=dma)

    def finalize(self, ctx):
        sems = {e: ctx.enter_context(self.nc.semaphore(f"s_{e}")) for e in ENGS}
        block = ctx.enter_context(self.nc.Block())
        final = {s: self.c[s] for s in ("sync", "gpsimd") if self.c[s] > 0}
        for eng in ENGS:
            ops = self.q[eng]
            if not ops and eng != "vector":
                continue

            def mk(ops, eng):
                def run(e):
                    for build, w, step in ops:
                        for s, v in w:
                            e.wait_ge(sems[s], v)
                        inst = build(e)
                        inst.then_inc(sems[eng], step)
                    if eng == "vector":
                        for s, v in final.items():
                            e.wait_ge(sems[s], v)
                return run
            getattr(block, eng)(mk(ops, eng))



class Banks:

    def __init__(self, p):
        self.p = p
        self.free = {}

    def wait(self, i):
        return [self.free[i]] if i in self.free else []

    def evac(self, i, eng):
        self.free[i] = (eng, self.p.c[eng])


class LNWork:
    def __init__(self, nc, ctx, p1, p2, tag=""):
        sb = lambda n, s, d: ctx.enter_context(nc.sbuf_tensor(n + tag, s, d))
        self.sq = sb("lnsq", [128, 8, 512], BF16)
        self.tmp = sb("lntmp", [128, 512], F32)
        self.m = sb("lnm", [1, 512], F32)
        self.e2 = sb("lne2", [1, 512], F32)
        self.m2 = sb("lnm2", [1, 512], BF16)
        self.var = sb("lnvar", [1, 512], F32)
        self.sd = sb("lnsd", [1, 512], F32)
        self.r = sb("lnr", [1, 512], F32)
        self.ones_col = sb("lnoc", [128, 1], F32)
        self.ones_bcol = sb("lnob", [128, 1], BF16)
        self.ones_row = sb("lnor", [1, 128], F32)
        self.ones_brow = sb("lnobr", [1, 128], BF16)
        self.rbf = sb("lnrbf", [1, 512], BF16)
        self.mbf = sb("lnmbf", [1, 512], BF16)
        self.p1 = p1
        self.p2 = p2

    def load(self, p, oc_dram, or_dram):
        p.op("sync", lambda e: e.dma_start(out=self.ones_col[:], in_=oc_dram[:]))
        p.op("sync", lambda e: e.dma_start(out=self.ones_row[:], in_=or_dram[:]))
        p.op("vector", lambda e: e.tensor_copy(out=self.ones_bcol[:], in_=self.ones_col[:]),
             deps=("sync",))
        p.op("vector", lambda e: e.tensor_copy(out=self.ones_brow[:], in_=self.ones_row[:]))


def emit_ln(p, w, TT, src, dst=None, resid=None, nchunks=8, dnorm=1024.0):
    vop = p.sop if TT < 64 else p.op
    for c in range(nchunks):
        p.op("scalar", lambda e, c=c: e.activation(
            out=w.sq[:, c, :TT], in_=src(c), func=AF.Square),
            deps=("sync", "vector", "tensor") if c == 0 else ())
    for c in range(nchunks):
        p.op("tensor", lambda e, c=c: e.matmul(
            w.p1[0:1, :TT], w.ones_col[:], src(c),
            start=(c == 0), stop=(c == nchunks - 1)),
            deps=("sync", "vector") if c == 0 else ())
    for c in range(nchunks):
        p.op("tensor", lambda e, c=c: e.matmul(
            w.p2[0:1, :TT], w.ones_bcol[:], w.sq[:, c, :TT],
            start=(c == 0), stop=(c == nchunks - 1)),
            deps=("scalar",) if c == 0 else ())
    pe = p.c["tensor"]
    vop("vector", lambda e: e.tensor_scalar_mul(out=w.m[:, :TT], in0=w.p1[0:1, :TT],
                                                 scalar1=1.0 / dnorm),
         deps=[("tensor", pe)])
    vop("vector", lambda e: e.tensor_scalar(out=w.e2[:, :TT], in0=w.p2[0:1, :TT],
                                             scalar1=1.0 / dnorm, scalar2=EPS,
                                             op0=ALU.mult, op1=ALU.add))
    vop("vector", lambda e: e.tensor_tensor(out=w.m2[:, :TT], in0=w.m[:, :TT],
                                             in1=w.m[:, :TT], op=ALU.mult))
    vop("vector", lambda e: e.tensor_tensor(out=w.var[:, :TT], in0=w.e2[:, :TT],
                                             in1=w.m2[:, :TT], op=ALU.subtract))
    vd = p.c["vector"]
    p.op("scalar", lambda e: e.activation(out=w.sd[:, :TT], in_=w.var[:, :TT],
                                          func=AF.Sqrt),
         deps=[("vector", vd)])
    sd = p.c["scalar"]
    vop("vector", lambda e: e.reciprocal(out=w.r[:, :TT], in_=w.sd[:, :TT]),
         deps=[("scalar", sd)])
    vop("vector", lambda e: e.tensor_copy(out=w.rbf[:, :TT], in_=w.r[:, :TT]))
    vop("vector", lambda e: e.tensor_copy(out=w.mbf[:, :TT], in_=w.m[:, :TT]))
    vd = p.c["vector"]
    p.op("tensor", lambda e: e.matmul(w.p1[:, :TT], w.ones_brow[:], w.rbf[:, :TT],
                                      start=True, stop=True),
         deps=[("vector", vd)])
    p.op("tensor", lambda e: e.matmul(w.p2[:, :TT], w.ones_brow[:], w.mbf[:, :TT],
                                      start=True, stop=True))
    pe = p.c["tensor"]
    for c in range(nchunks):
        vop("vector", lambda e, c=c: e.tensor_tensor(
            out=w.tmp[:, :TT], in0=src(c), in1=w.p2[:, :TT], op=ALU.subtract),
            deps=[("tensor", pe)] if c == 0 else ())
        if resid is None:
            vop("vector", lambda e, c=c: e.tensor_tensor(
                out=dst(c), in0=w.tmp[:, :TT], in1=w.p1[:, :TT], op=ALU.mult))
        else:
            vop("vector", lambda e, c=c: e.tensor_tensor(
                out=w.tmp[:, :TT], in0=w.tmp[:, :TT], in1=w.p1[:, :TT], op=ALU.mult))
            vop("vector", lambda e, c=c: e.tensor_tensor(
                out=resid(c), in0=resid(c), in1=w.tmp[:, :TT], op=ALU.add))



def emit_attn(nc, ctx, p, lnw, PS, d, tag="A"):
    xb = d["xb"]; xo = d["xo"]; wq = d["wq"]; wk = d["wk"]; wv = d["wv"]; wp = d["wp"]; bp2 = d["bp2"]; onc = d["onc"]; onr = d["onr"]; mks = d["mks"]; ali = d["ali"]; x1 = d["x1"]; qTd = d["qTd"]; kTd = d["kTd"]; vd = d["vd"]; ctxd = d["ctxd"]
    with ExitStack() as ectx:
        sb = lambda n, sh, dt_: ectx.enter_context(nc.sbuf_tensor(n + tag, sh, dt_))

        xt = sb("xt", [128, 8, 512], F32)
        xn = sb("xnt", [128, 8, 512], BF16)
        wstage = sb("wstage", [128, 1024], F32)
        w_bf = sb("w_bf", [128, 2, 8, 1024], BF16)
        evbuf = sb("evbuf", [128, 1024], BF16)
        mix = sb("mix", [128, 8, 512], F32)
        kT_h = sb("kT_h", [64, 2, T], BF16)
        q_h = sb("q_h", [64, 2, CTOK], BF16)
        vaug = sb("vaug", [128, 2, 32, 65], BF16)
        probs = sb("probs", [128, 4, 512], BF16)
        abias = sb("abias", [128, 256], F32)
        mask_sb = sb("mask_sb", [128, 4, 8, 512], BF16)
        cbuf = sb("cbuf", [128, 8, 512], BF16)
        csc = sb("csc", [64, 512], F32)
        ctile = sb("ctile", [64, 512], BF16)
        rec = sb("rec", [1, 512], F32)

        for qb in range(4):
            for j in range(8):
                p.op("sync", lambda e, qb=qb, j=j: e.dma_start(
                    out=mask_sb[:, qb, j, :], in_=mks[qb, j]))
        p.op("vector", lambda e: e.memset(vaug[:, :, :, 64:65], 1.0))

        def load_w(dram, slot):
            for kc in range(8):
                p.op("sync", lambda e, kc=kc: e.dma_start(
                    out=wstage[:, :], in_=dram[kc * 128:(kc + 1) * 128, :]),
                    deps=("vector", "tensor"))
                p.op("vector", lambda e, kc=kc, slot=slot: e.tensor_copy(
                    out=w_bf[:, slot, kc, :], in_=wstage[:, :]), deps=("sync",))

        def ln_tile(src_dram, t0):
            for c in range(8):
                p.op("sync", lambda e, c=c: e.dma_start(
                    out=xt[:, c, :], in_=src_dram[c * 128:(c + 1) * 128, t0:t0 + 512]),
                    deps=("scalar", "tensor", "vector") if c == 0 else ())
            emit_ln(p, lnw, 512, src=lambda c: xt[:, c, :],
                    dst=lambda c: xn[:, c, :])

        def proj_heads(dram_out, o0, wslot, scale):
            for oc in range(8):
                bank = PS[2 + (oc % 2)]
                vdn, scn, gdn = p.c["vector"], p.c["scalar"], p.c["gpsimd"]
                for kc in range(8):
                    p.op("tensor", lambda e, kc=kc, oc=oc, bank=bank, wslot=wslot: e.matmul(
                        bank[:, :], w_bf[:, wslot, kc, oc * 128:(oc + 1) * 128],
                        xn[:, kc, :], start=(kc == 0), stop=(kc == 7)),
                        deps=[("vector", vdn), ("scalar", scn)] if kc == 0 else ())
                pe = p.c["tensor"]
                p.op("scalar", lambda e, bank=bank: e.activation(
                    out=evbuf[:, :512], in_=bank[:, :], func=AF.Copy, scale=scale),
                    deps=[("tensor", pe), ("gpsimd", gdn)])
                sc = p.c["scalar"]
                for half in range(2):
                    h = 2 * oc + half
                    p.op("gpsimd", lambda e, h=h, half=half: e.dma_start(
                        out=dram_out[h, :, o0:o0 + 512],
                        in_=evbuf[half * 64:(half + 1) * 64, :512]),
                        deps=[("scalar", sc)], dma=True)

        load_w(wk, 0)
        load_w(wv, 1)
        for t0 in range(0, T, 512):
            ln_tile(xb, t0)
            proj_heads(kTd, t0, 0, 1.0)
            for tc in range(4):
                for eh in range(2):
                    bank = PS[2 + eh]
                    vdn, scn, gdn = p.c["vector"], p.c["scalar"], p.c["gpsimd"]
                    for kc in range(8):
                        p.op("tensor", lambda e, kc=kc, tc=tc, eh=eh, bank=bank: e.matmul(
                            bank[:, :], xn[:, kc, tc * 128:(tc + 1) * 128],
                            w_bf[:, 1, kc, eh * 512:(eh + 1) * 512],
                            start=(kc == 0), stop=(kc == 7)),
                            deps=[("vector", vdn), ("scalar", scn)] if kc == 0 else ())
                    pe = p.c["tensor"]
                    p.op("scalar", lambda e, bank=bank, eh=eh: e.activation(
                        out=evbuf[:, eh * 512:(eh + 1) * 512], in_=bank[:, :],
                        func=AF.Copy), deps=[("tensor", pe), ("gpsimd", gdn)])
                    sc = p.c["scalar"]
                    for hh in range(8):
                        h = eh * 8 + hh
                        p.op("gpsimd", lambda e, h=h, hh=hh, eh=eh, t0=t0, tc=tc: e.dma_start(
                            out=vd[h, t0 + tc * 128:t0 + (tc + 1) * 128, :],
                            in_=evbuf[:, eh * 512 + hh * 64:eh * 512 + (hh + 1) * 64]),
                            deps=[("scalar", sc)], dma=True)

        load_w(wq, 0)
        for s in range(4):
            ln_tile(xo, s * 512)
            proj_heads(qTd, s * 512, 0, 0.125)

        banksA = Banks(p)
        probs_rd = {}
        blk_rd = {}
        kv_rd = {}
        for _bi in (2, 4, 6, 7):
            banksA.free[_bi] = ("scalar", p.c["scalar"])
        _v0 = p.c["vector"]
        _s0 = p.c["scalar"]

        def load_head(h):
            sl = h % 2
            deps = [("tensor", kv_rd[sl])] if sl in kv_rd else [("tensor", p.c["tensor"]), ("gpsimd", p.c["gpsimd"]), ("scalar", _s0), ("vector", _v0)]
            p.op("sync", lambda e, h=h, sl=sl: e.dma_start(
                out=kT_h[:, sl, :], in_=kTd[h]), deps=deps)
            p.op("sync", lambda e, h=h, sl=sl: e.dma_start(
                out=q_h[:, sl, :], in_=qTd[h]))
            p.op("sync", lambda e, h=h, sl=sl: e.dma_start(
                out=abias[:, sl * 128:(sl + 1) * 128], in_=ali[h]))
            for tc in range(32):
                p.op("sync", lambda e, h=h, sl=sl, tc=tc: e.dma_start(
                    out=vaug[:, sl, tc, 0:64], in_=vd[h, tc * 128:(tc + 1) * 128, :]))
            return p.c["sync"]

        ld_of = {0: load_head(0)}
        for h in range(NH):
            sl = h % 2
            ld = ld_of.pop(h)
            if h + 1 < NH:
                ld_of[h + 1] = load_head(h + 1)
            for qb in range(4):
                qo = qb * 512
                nkt = NKT_PAD[qb]
                pending = None

                def emit_av(gg, slot, dv, nkt=nkt, sl=sl):
                    for j in range(2):
                        kt = gg + j
                        avdeps = [("vector", dv)]
                        if kt == 0:
                            avdeps += banksA.wait(6)
                        p.op("tensor", lambda e, kt=kt, slot=slot + j, sl=sl, nkt=nkt: e.matmul(
                            PS[6][0:65, :], vaug[:, sl, kt, :], probs[:, slot, :],
                            start=(kt == 0), stop=(kt == nkt - 1)),
                            deps=avdeps)
                    probs_rd[slot] = p.c["tensor"]

                for g in range(0, nkt, 2):
                    blk = (g // 2) % 2
                    base = 2 + blk * 2
                    for j in range(2):
                        kt = g + j
                        stdeps = ([("sync", ld)] + banksA.wait(base)) if j == 0 else ()
                        p.op("tensor", lambda e, kt=kt, qo=qo, sl=sl, bi=base + j: e.matmul(
                            PS[8][:, bi * 512:(bi + 1) * 512],
                            kT_h[:, sl, kt * 128:(kt + 1) * 128],
                            q_h[:, sl, qo:qo + 512], start=True, stop=True),
                            deps=stdeps)
                    pe = p.c["tensor"]
                    slot = g % 4
                    pdeps = [("tensor", pe)]
                    if slot in probs_rd:
                        pdeps.append(("tensor", probs_rd[slot]))
                    p.op("scalar", lambda e, slot=slot, base=base: e.activation(
                        out=probs[:, slot:slot + 2, :],
                        in_=PS[8][:, base * 512:(base + 2) * 512], func=AF.Exp),
                        deps=pdeps)
                    banksA.evac(base, "scalar")
                    sg = p.c["scalar"]
                    for j in range(2):
                        kt = g + j
                        p.op("vector", lambda e, qb=qb, kt=kt, slot=slot + j, sl=sl: e.tensor_scalar_mul(
                            out=probs[:, slot, :], in0=probs[:, slot, :],
                            scalar1=abias[:, sl * 128 + qb * 32 + kt:sl * 128 + qb * 32 + kt + 1]),
                            deps=[("scalar", sg)] if j == 0 else ())
                        if kt in DIAG_SLOTS[qb]:
                            jj = DIAG_SLOTS[qb].index(kt)
                            p.op("vector", lambda e, slot=slot + j, qb=qb, jj=jj: e.tensor_tensor(
                                out=probs[:, slot, :], in0=probs[:, slot, :],
                                in1=mask_sb[:, qb, jj, :], op=ALU.mult))
                    dv = p.c["vector"]
                    if pending is not None:
                        emit_av(*pending)
                    pending = (g, slot, dv)
                emit_av(*pending)
                pe = p.c["tensor"]
                p.op("vector", lambda e: e.reciprocal(out=rec[:, :], in_=PS[6][64:65, :]),
                     deps=[("tensor", pe)])
                vv = p.c["vector"]
                p.op("scalar", lambda e: e.activation(out=csc[:, :], in_=PS[6][0:64, :],
                                                      func=AF.Copy),
                     deps=[("tensor", pe)])
                banksA.evac(6, "scalar")
                p.op("tensor", lambda e: e.matmul(PS[7][0:64, :], lnw.ones_row[:, 0:64],
                                                  rec[:, :], start=True, stop=True),
                     deps=[("vector", vv)] + banksA.wait(7))
                pe = p.c["tensor"]
                sc = p.c["scalar"]
                p.op("vector", lambda e: e.tensor_tensor(
                    out=ctile[:, :], in0=csc[:, :], in1=PS[7][0:64, :], op=ALU.mult),
                    deps=[("tensor", pe), ("scalar", sc), ("gpsimd", p.c["gpsimd"])])
                banksA.evac(7, "vector")
                vv = p.c["vector"]
                p.op("gpsimd", lambda e, h=h, qo=qo: e.dma_start(
                    out=ctxd[h * 64:(h + 1) * 64, qo:qo + 512], in_=ctile[:, :]),
                    deps=[("vector", vv)], dma=True)
            kv_rd[sl] = p.c["tensor"]
        load_w(wp, 0)
        bvec = sb("bvec", [128, 8], F32)
        p.op("sync", lambda e: e.dma_start(out=bvec[:, :], in_=bp2[:]))
        for s in range(4):
            t0 = s * 512
            for c in range(8):
                p.op("sync", lambda e, c=c, t0=t0: e.dma_start(
                    out=cbuf[:, c, :], in_=ctxd[c * 128:(c + 1) * 128, t0:t0 + 512]),
                    deps=("gpsimd", "tensor") if c == 0 else ())
            for c in range(8):
                p.op("sync", lambda e, c=c, t0=t0: e.dma_start(
                    out=xt[:, c, :], in_=xo[c * 128:(c + 1) * 128, t0:t0 + 512]),
                    deps=("vector", "gpsimd") if c == 0 else ())
            for oc in range(8):
                bank = PS[2 + (oc % 2)]
                vdn, scn = p.c["vector"], p.c["scalar"]
                for kc in range(8):
                    p.op("tensor", lambda e, kc=kc, oc=oc, bank=bank: e.matmul(
                        bank[:, :], w_bf[:, 0, kc, oc * 128:(oc + 1) * 128],
                        cbuf[:, kc, :], start=(kc == 0), stop=(kc == 7)),
                        deps=[("sync", p.c["sync"]), ("vector", vdn),
                              ("scalar", scn)] if kc == 0 else ())
                pe = p.c["tensor"]
                p.op("vector", lambda e, oc=oc, bank=bank: e.tensor_scalar_add(
                    out=mix[:, oc, :], in0=bank[:, :], scalar1=bvec[:, oc:oc + 1]),
                    deps=[("tensor", pe)])
            emit_ln(p, lnw, 512, src=lambda c: mix[:, c, :],
                    resid=lambda c: xt[:, c, :])
            vv = p.c["vector"]
            for c in range(8):
                p.op("gpsimd", lambda e, c=c, t0=t0: e.dma_start(
                    out=x1[c * 128:(c + 1) * 128, t0:t0 + 512], in_=xt[:, c, :]),
                    deps=[("vector", vv)], dma=True)



def build_attn():
    nc = bass.Bass("TRN2", target_bir_lowering=False)
    xb = nc.dram_tensor("xb", [D, T], F32, kind="ExternalInput")
    xo = nc.dram_tensor("xo", [D, CTOK], F32, kind="ExternalInput")
    wq = nc.dram_tensor("wq", [D, D], F32, kind="ExternalInput")
    wk = nc.dram_tensor("wk", [D, D], F32, kind="ExternalInput")
    wv = nc.dram_tensor("wv", [D, D], F32, kind="ExternalInput")
    wp = nc.dram_tensor("wp", [D, D], F32, kind="ExternalInput")
    bp2 = nc.dram_tensor("bp2", [128, 8], F32, kind="ExternalInput")
    onc = nc.dram_tensor("onc", [128, 1], F32, kind="ExternalInput")
    onr = nc.dram_tensor("onr", [1, 128], F32, kind="ExternalInput")
    mks = nc.dram_tensor("mks", [4, 8, 128, 512], BF16, kind="ExternalInput")
    ali = nc.dram_tensor("ali", [NH, 128, 128], F32, kind="ExternalInput")
    x1 = nc.dram_tensor("x1", [D, CTOK], F32, kind="ExternalOutput")
    qTd = nc.dram_tensor("qTd", [NH, DH, CTOK], BF16)
    kTd = nc.dram_tensor("kTd", [NH, DH, T], BF16)
    vd = nc.dram_tensor("vd", [NH, T, DH], BF16)
    ctxd = nc.dram_tensor("ctxd", [D, CTOK], BF16)

    d = dict(xb=xb, xo=xo, wq=wq, wk=wk, wv=wv, wp=wp, bp2=bp2, onc=onc, onr=onr, mks=mks, ali=ali, x1=x1, qTd=qTd, kTd=kTd, vd=vd, ctxd=ctxd)
    with ExitStack() as ctx:
        psb = ctx.enter_context(nc.psum_tensor("psb", [128, 4096], F32))
        PS = [psb[:, i * 512:(i + 1) * 512] for i in range(8)] + [psb]
        p = Prog(nc)
        lnw = LNWork(nc, ctx, PS[0], PS[1])
        lnw.load(p, d["onc"], d["onr"])
        emit_attn(nc, ctx, p, lnw, PS, d)
        p.finalize(ctx)
    return nc



def attn_consts(qstarts):
    ones_col = np.ones((128, 1), dtype=np.float32)
    ones_row = np.ones((1, 128), dtype=np.float32)
    mks = np.ones((4, 8, 128, 512), dtype=np.float32)
    i = np.arange(128)[:, None]
    j = np.arange(512)[None, :]
    for qb, q0 in enumerate(qstarts):
        kd = q0 // 128
        for jj, kt in enumerate(DIAG_SLOTS[qb]):
            if kd <= kt < kd + 4:
                a = kt - kd
                mks[qb, jj] = ((a * 128 + i) <= j).astype(np.float32)
    slopes = 2.0 ** (-8.0 * np.arange(1, NH + 1, dtype=np.float64) / NH)
    ali = np.zeros((NH, 128, 128), dtype=np.float64)
    for h in range(NH):
        for qb, q0 in enumerate(qstarts):
            for kt in range(NKT_PAD[qb]):
                if kt * 128 >= q0 + 512:
                    ali[h, :, qb * 32 + kt] = NEG
                else:
                    k = kt * 128 + np.arange(128, dtype=np.float64)
                    ali[h, :, qb * 32 + kt] = slopes[h] * (k - q0 - 511.0) / 8.0
    ali = np.exp(ali).astype(np.float32)
    return {"onc": ones_col, "onr": ones_row, "mks": mks.astype(BF), "ali": ali}


def chunkcol(v):
    v = np.asarray(v, dtype=np.float32)
    return np.ascontiguousarray(v.reshape(-1, 128).T)



def _fm(x):
    return np.ascontiguousarray(np.transpose(np.asarray(x, dtype=np.float32), (0, 2, 1)))


def _own_cols(xfm, c):
    b, qr = core_quarters(c)
    return np.ascontiguousarray(np.concatenate(
        [xfm[b][:, a:bb] for (a, bb) in qr], axis=1))


def _scatter_back(shards):
    out = np.empty((B, T, D), dtype=np.float32)
    for c in range(NC):
        b, qr = core_quarters(c)
        off = 0
        for (a, bb) in qr:
            w = bb - a
            out[b, a:bb, :] = shards[c][:, off:off + w].T
            off += w
    return out


def run_attn(inputs, xfull):
    xfm = _fm(xfull)
    wq = np.asarray(inputs["L0_attn_W_query"], dtype=np.float32)
    wk = np.asarray(inputs["L0_attn_W_key"], dtype=np.float32)
    wv = np.asarray(inputs["L0_attn_W_value"], dtype=np.float32)
    wp = np.asarray(inputs["L0_attn_W_proj"], dtype=np.float32)
    bp2 = chunkcol(inputs["L0_attn_b_proj"])
    in_maps = []
    for c in range(NC):
        b, _ = core_quarters(c)
        cst = attn_consts(core_qstarts(c))
        in_maps.append({"xb": xfm[b], "xo": _own_cols(xfm, c),
                        "wq": wq, "wk": wk, "wv": wv, "wp": wp, "bp2": bp2,
                        **cst})
    nc = build_attn()
    res = run_bass_kernel_spmd(nc, in_maps, core_ids=list(range(NC)))
    return _scatter_back([np.asarray(r["x1"]) for r in res.results])


def kernel(**inputs):
    x = np.asarray(inputs["L0_attn_x"], dtype=np.float32)
    if FUSED:
        x4 = run_fused(inputs, x)
    else:
        x1 = run_attn(inputs, x)
        x2 = run_moe(inputs, x1)
        x3 = run_mamba(inputs, x2)
        x4 = run_tern(inputs, x3)
    return np.ascontiguousarray(x4.astype(np.float32))



def emit_moe(nc, ctx, p, lnw, PS, d, tag="M"):
    xo = d["xo"]; gwt = d["gwt"]; w1t = d["w1t"]; w2t = d["w2t"]; onc = d["onc"]; onr = d["onr"]; idf = d["idf"]; seld = d["seld"]; x2 = d["x2"]
    with ExitStack() as ectx:
        sb = lambda n, sh, dt_: ectx.enter_context(nc.sbuf_tensor(n + tag, sh, dt_))

        xt = sb("xt", [128, 8, 512], F32)
        xn = sb("xn", [128, 8, CTOK], BF16)
        zac = sb("zac", [128, 8, CTOK], F32)
        w1bf = sb("w1bf", [128, 8, 1536], BF16)
        w2bf = sb("w2bf", [128, 6, 1024], BF16)
        wstage = sb("wstage", [128, 1024], F32)
        ybf = sb("ybf", [128, 6, 512], BF16)
        gbf = sb("gbf", [128, 6, 512], BF16)
        ygbf = sb("ygbf", [128, 6, 512], BF16)
        gw = sb("gw", [128, 8, 8], BF16)
        gstage = sb("gstage", [128, 8], F32)
        ids = sb("ids", [128, 128], F32)
        s8 = sb("s8", [128, 8], F32)
        pn = sb("pn", [128, 8], F32)
        t8 = sb("t8", [128, 8], F32)
        ge8 = sb("ge8", [128, 8], F32)
        w8 = sb("w8", [128, 8], F32)
        ncol = sb("ncol", [128, 1], F32)
        scol = sb("scol", [128, 1], F32)
        wE = sb("wE", [10, CTOK], BF16)
        wbc = sb("wbc", [128, CTOK], BF16)
        sel10 = sb("sel10", [10, 1280], BF16)

        p.op("sync", lambda e: e.dma_start(out=ids[:], in_=idf[:]))
        p.op("sync", lambda e: e.dma_start(out=sel10[:], in_=seld[:]))
        for kc in range(8):
            p.op("sync", lambda e, kc=kc: e.dma_start(
                out=gstage[:, :], in_=gwt[kc * 128:(kc + 1) * 128, :]),
                deps=("vector",))
            p.op("vector", lambda e, kc=kc: e.tensor_copy(
                out=gw[:, kc, :], in_=gstage[:, :]), deps=("sync",))

        for s in range(4):
            t0 = s * 512
            for c in range(8):
                p.op("sync", lambda e, c=c, t0=t0: e.dma_start(
                    out=xt[:, c, :], in_=xo[c * 128:(c + 1) * 128, t0:t0 + 512]),
                    deps=("scalar", "tensor", "vector") if c == 0 else ())
            emit_ln(p, lnw, 512, src=lambda c: xt[:, c, :],
                    dst=lambda c, t0=t0: xn[:, c, t0:t0 + 512])

        p.op("vector", lambda e: e.memset(wE[0:10, :], 1.0))
        for st in range(16):
            c0 = st * 128
            vdn = p.c["vector"]
            for kc in range(8):
                p.op("tensor", lambda e, kc=kc, c0=c0: e.matmul(
                    PS[2][:, 0:8], xn[:, kc, c0:c0 + 128], gw[:, kc, :],
                    start=(kc == 0), stop=(kc == 7)),
                    deps=[("vector", vdn), ("scalar", p.c["scalar"])] if kc == 0 else ())
            pe = p.c["tensor"]
            p.op("scalar", lambda e: e.activation(out=s8[:, :], in_=PS[2][:, 0:8],
                                                  func=AF.Copy), deps=[("tensor", pe)])
            sc = p.c["scalar"]
            p.sop("vector", lambda e: e.reduce_max(out=ncol[:, :], in_=s8[:, :],
                                                   axis=AX, negate=True),
                  deps=[("scalar", sc)])
            vv = p.c["vector"]
            p.op("scalar", lambda e: e.activation(out=pn[:, :], in_=s8[:, :], func=AF.Exp,
                                                  bias=ncol[:, :], accum_out=scol[:, :]),
                 deps=[("vector", vv)])
            sc = p.c["scalar"]
            p.sop("vector", lambda e: e.reciprocal(out=ncol[:, :], in_=scol[:, :]),
                  deps=[("scalar", sc)])
            p.sop("vector", lambda e: e.tensor_scalar_mul(out=pn[:, :], in0=pn[:, :],
                                                          scalar1=ncol[:, :]))
            p.sop("vector", lambda e: e.reduce_max(out=ncol[:, :], in_=pn[:, :],
                                                   axis=AX, negate=True))
            p.sop("vector", lambda e: e.tensor_scalar_add(out=t8[:, :], in0=pn[:, :],
                                                          scalar1=ncol[:, :]))
            p.sop("vector", lambda e: e.tensor_single_scalar(out=ge8[:, :], in_=t8[:, :],
                                                             scalar=0.0, op=ALU.is_ge))
            p.sop("vector", lambda e: e.scalar_tensor_tensor(out=t8[:, :], in0=ge8[:, :],
                                                             scalar=NEG, in1=pn[:, :],
                                                             op0=ALU.mult, op1=ALU.add))
            p.sop("vector", lambda e: e.reduce_max(out=ncol[:, :], in_=t8[:, :],
                                                   axis=AX, negate=True))
            p.sop("vector", lambda e: e.tensor_scalar_add(out=t8[:, :], in0=pn[:, :],
                                                          scalar1=ncol[:, :]))
            p.sop("vector", lambda e: e.tensor_single_scalar(out=ge8[:, :], in_=t8[:, :],
                                                             scalar=0.0, op=ALU.is_ge))
            p.sop("vector", lambda e: e.tensor_tensor(out=w8[:, :], in0=pn[:, :],
                                                      in1=ge8[:, :], op=ALU.mult))
            vv = p.c["vector"]
            p.op("tensor", lambda e: e.transpose(PS[3][0:8, 0:128], w8[:, :], ids[:]),
                 deps=[("vector", vv), ("scalar", p.c["scalar"])])
            pe = p.c["tensor"]
            p.op("scalar", lambda e, c0=c0: e.activation(
                out=wE[0:8, c0:c0 + 128], in_=PS[3][0:8, 0:128], func=AF.Copy),
                deps=[("tensor", pe)])

        for c in range(8):
            p.op("vector", lambda e, c=c: e.memset(zac[:, c, :], 0.0))
        gate_ready = p.c["vector"]
        banks = Banks(p)
        for _bi in range(7):
            banks.free[_bi] = ("vector", p.c["vector"])

        for ex in range(10):
            for kc in range(8):
                for hf in range(2):
                    p.op("sync", lambda e, ex=ex, kc=kc, hf=hf: e.dma_start(
                        out=wstage[:, 0:768],
                        in_=w1t[ex, kc * 128:(kc + 1) * 128, hf * 768:(hf + 1) * 768]),
                        deps=("vector", "tensor"))
                    p.op("vector", lambda e, kc=kc, hf=hf: e.tensor_copy(
                        out=w1bf[:, kc, hf * 768:(hf + 1) * 768],
                        in_=wstage[:, 0:768]), deps=("sync",))
            for kc in range(6):
                p.op("sync", lambda e, ex=ex, kc=kc: e.dma_start(
                    out=wstage[:, 0:1024], in_=w2t[ex, kc * 128:(kc + 1) * 128, :]),
                    deps=("vector", "tensor"))
                p.op("vector", lambda e, kc=kc: e.tensor_copy(
                    out=w2bf[:, kc, :], in_=wstage[:, 0:1024]), deps=("sync",))
            w_ready = p.c["vector"]
            wb_dep = []
            for s in range(4):
                p.op("tensor", lambda e, ex=ex, s=s: e.matmul(
                    PS[6][:, :], sel10[:, ex * 128:(ex + 1) * 128],
                    wE[:, s * 512:(s + 1) * 512], start=True, stop=True),
                    deps=banks.wait(6) + ([("vector", gate_ready)] if ex == 0 and s == 0 else []))
                pe = p.c["tensor"]
                p.op("scalar", lambda e, s=s: e.activation(
                    out=wbc[:, s * 512:(s + 1) * 512], in_=PS[6][:, :], func=AF.Copy),
                    deps=[("tensor", pe)])
                banks.evac(6, "scalar")
            wbc_ready = p.c["scalar"]
            for s in range(4):
                t0 = s * 512
                ybf_dep = None
                for oc in range(12):
                    bi = oc % 4
                    bank = PS[bi]
                    for kc in range(8):
                        p.op("tensor", lambda e, kc=kc, oc=oc, t0=t0, bank=bank: e.matmul(
                            bank[:, :], w1bf[:, kc, oc * 128:(oc + 1) * 128],
                            xn[:, kc, t0:t0 + 512],
                            start=(kc == 0), stop=(kc == 7)),
                            deps=(banks.wait(bi) + [("vector", w_ready)]) if kc == 0 else ())
                    pe = p.c["tensor"]
                    if oc < 6:
                        p.op("vector", lambda e, oc=oc, t0=t0, bank=bank: e.tensor_tensor(
                            out=ybf[:, oc, :], in0=wbc[:, t0:t0 + 512], in1=bank[:, :],
                            op=ALU.mult),
                            deps=[("tensor", pe), ("scalar", wbc_ready)])
                        banks.evac(bi, "vector")
                    else:
                        p.op("scalar", lambda e, oc=oc, bank=bank: e.activation(
                            out=gbf[:, oc - 6, :], in_=bank[:, :], func=AF.Silu),
                            deps=[("tensor", pe)])
                        banks.evac(bi, "scalar")
                sg = p.c["scalar"]
                vg = p.c["vector"]
                for j in range(6):
                    p.op("gpsimd", lambda e, j=j: e.tensor_tensor(
                        out=ygbf[:, j, :], in0=ybf[:, j, :], in1=gbf[:, j, :],
                        op=ALU.mult),
                        deps=[("scalar", sg), ("vector", vg)] if j == 0 else ())
                yg_ready = ("gpsimd", p.c["gpsimd"])
                for oc in range(8):
                    bi = 4 + (oc % 2)
                    bank = PS[bi]
                    for kc in range(6):
                        p.op("tensor", lambda e, kc=kc, oc=oc, bank=bank: e.matmul(
                            bank[:, :], w2bf[:, kc, oc * 128:(oc + 1) * 128],
                            ygbf[:, kc, :], start=(kc == 0), stop=(kc == 5)),
                            deps=(banks.wait(bi) + [yg_ready]) if kc == 0 else ())
                    pe = p.c["tensor"]
                    p.op("vector", lambda e, oc=oc, t0=t0, bank=bank: e.tensor_tensor(
                        out=zac[:, oc, t0:t0 + 512], in0=zac[:, oc, t0:t0 + 512],
                        in1=bank[:, :], op=ALU.add), deps=[("tensor", pe)])
                    banks.evac(bi, "vector")

        for s in range(4):
            t0 = s * 512
            for c in range(8):
                p.op("sync", lambda e, c=c, t0=t0: e.dma_start(
                    out=xt[:, c, :], in_=xo[c * 128:(c + 1) * 128, t0:t0 + 512]),
                    deps=("vector", "gpsimd") if c == 0 else ())
            emit_ln(p, lnw, 512, src=lambda c, t0=t0: zac[:, c, t0:t0 + 512],
                    resid=lambda c: xt[:, c, :])
            vv = p.c["vector"]
            for c in range(8):
                p.op("gpsimd", lambda e, c=c, t0=t0: e.dma_start(
                    out=x2[c * 128:(c + 1) * 128, t0:t0 + 512], in_=xt[:, c, :]),
                    deps=[("vector", vv)], dma=True)



def build_moe():
    nc = bass.Bass("TRN2", target_bir_lowering=False)
    xo = nc.dram_tensor("xo", [D, CTOK], F32, kind="ExternalInput")
    gwt = nc.dram_tensor("gwt", [D, 8], F32, kind="ExternalInput")
    w1t = nc.dram_tensor("w1t", [10, D, 1536], F32, kind="ExternalInput")
    w2t = nc.dram_tensor("w2t", [10, 768, D], F32, kind="ExternalInput")
    onc = nc.dram_tensor("onc", [128, 1], F32, kind="ExternalInput")
    onr = nc.dram_tensor("onr", [1, 128], F32, kind="ExternalInput")
    idf = nc.dram_tensor("idf", [128, 128], F32, kind="ExternalInput")
    seld = nc.dram_tensor("seld", [10, 1280], BF16, kind="ExternalInput")
    x2 = nc.dram_tensor("x2", [D, CTOK], F32, kind="ExternalOutput")

    d = dict(xo=xo, gwt=gwt, w1t=w1t, w2t=w2t, onc=onc, onr=onr, idf=idf, seld=seld, x2=x2)
    with ExitStack() as ctx:
        psb = ctx.enter_context(nc.psum_tensor("psb", [128, 4096], F32))
        PS = [psb[:, i * 512:(i + 1) * 512] for i in range(8)] + [psb]
        p = Prog(nc)
        lnw = LNWork(nc, ctx, PS[0], PS[1])
        lnw.load(p, d["onc"], d["onr"])
        emit_moe(nc, ctx, p, lnw, PS, d)
        p.finalize(ctx)
    return nc


def moe_weights(inputs):
    w1 = np.asarray(inputs["L0_ffn_W1"], dtype=np.float32)
    w2 = np.asarray(inputs["L0_ffn_W2"], dtype=np.float32)
    ws1 = np.asarray(inputs["L0_ffn_Ws1"], dtype=np.float32)
    ws2 = np.asarray(inputs["L0_ffn_Ws2"], dtype=np.float32)
    ws1t = ws1.T
    ws2t = ws2.T
    w1t = np.zeros((10, D, 1536), dtype=np.float32)
    w2t = np.zeros((10, 768, D), dtype=np.float32)
    for e in range(8):
        w1t[e] = w1[e].T
        w2t[e] = w2[e].T
    w1t[8] = np.concatenate([ws1t[:, 0:768], ws1t[:, 1536:2304]], axis=1)
    w1t[9] = np.concatenate([ws1t[:, 768:1536], ws1t[:, 2304:3072]], axis=1)
    w2t[8] = ws2t[0:768]
    w2t[9] = ws2t[768:1536]
    gwt = np.ascontiguousarray(np.asarray(inputs["L0_ffn_gate_w"], np.float32).T)
    return gwt, np.ascontiguousarray(w1t), np.ascontiguousarray(w2t)


def run_moe(inputs, x1full):
    xfm = _fm(x1full)
    gwt, w1t, w2t = moe_weights(inputs)
    onc = np.ones((128, 1), dtype=np.float32)
    onr = np.ones((1, 128), dtype=np.float32)
    idf = np.eye(128, dtype=np.float32)
    seld = np.zeros((10, 1280), dtype=BF)
    for ex in range(10):
        seld[ex, ex * 128:(ex + 1) * 128] = 1.0
    in_maps = []
    for c in range(NC):
        in_maps.append({"xo": _own_cols(xfm, c), "gwt": gwt, "w1t": w1t,
                        "w2t": w2t, "onc": onc, "onr": onr, "idf": idf,
                        "seld": seld})
    nc = build_moe()
    res = run_bass_kernel_spmd(nc, in_maps, core_ids=list(range(NC)))
    return _scatter_back([np.asarray(r["x2"]) for r in res.results])



def emit_mpre(nc, ctx, p, lnw, PS, d, tag="P"):
    xo = d["xo"]; xh = d["xh"]; ipw = d["ipw"]; ipb2 = d["ipb2"]; cw2 = d["cw2"]; cb2 = d["cb2"]; hfl = d["hfl"]; onc = d["onc"]; onr = d["onr"]; u_d = d["u_d"]; sz_d = d["sz_d"]
    with ExitStack() as ectx:
        sb = lambda n, sh, dt_: ectx.enter_context(nc.sbuf_tensor(n + tag, sh, dt_))
        xt = sb("xt", [128, 8, 512], F32)
        xn = sb("xn", [128, 8, 512], BF16)
        ipbf = sb("ipbf", [128, 8, 4096], BF16)
        wstage = sb("wstage", [128, 4096], F32)
        xbuf = sb("xbuf", [128, 16, 515], F32)
        bias = sb("bias", [128, 32], F32)
        cwv = sb("cwv", [128, 16, 4], F32)
        cbv = sb("cbv", [128, 16], F32)
        acc = sb("acc", [128, 512], F32)
        hflv = sb("hflv", [128, 2], F32)
        ubf = sb("ubf", [128, 512], BF16)
        szbf = sb("szbf", [128, 512], BF16)

        p.op("sync", lambda e: e.dma_start(out=bias[:], in_=ipb2[:]))
        p.op("sync", lambda e: e.dma_start(out=cwv[:], in_=cw2[:]))
        p.op("sync", lambda e: e.dma_start(out=cbv[:], in_=cb2[:]))
        p.op("sync", lambda e: e.dma_start(out=hflv[:], in_=hfl[:]))
        for kc in range(8):
            p.op("sync", lambda e, kc=kc: e.dma_start(
                out=wstage[:, :], in_=ipw[kc * 128:(kc + 1) * 128, :]),
                deps=("vector",))
            p.op("vector", lambda e, kc=kc: e.tensor_copy(
                out=ipbf[:, kc, :], in_=wstage[:, :]), deps=("sync",))

        def ln_tile(src_dram, t0, TT):
            for c in range(8):
                p.op("sync", lambda e, c=c, t0=t0, TT=TT: e.dma_start(
                    out=xt[:, c, :TT], in_=src_dram[c * 128:(c + 1) * 128, t0:t0 + TT]),
                    deps=("scalar", "tensor", "vector") if c == 0 else ())
            emit_ln(p, lnw, TT, src=lambda c, TT=TT: xt[:, c, :TT],
                    dst=lambda c, TT=TT: xn[:, c, :TT])

        for ck in range(2):
            ln_tile(xh, ck * 3, 3)
            for oc in range(16):
                vdn = p.c["vector"]
                for kc in range(8):
                    p.op("tensor", lambda e, kc=kc, oc=oc: e.matmul(
                        PS[2][:, 0:3], ipbf[:, kc, oc * 128:(oc + 1) * 128],
                        xn[:, kc, 0:3], start=(kc == 0), stop=(kc == 7)),
                        deps=[("vector", vdn), ("scalar", p.c["scalar"])] if kc == 0 else ())
                pe = p.c["tensor"]
                p.op("vector", lambda e, oc=oc: e.tensor_scalar_add(
                    out=xbuf[:, oc, 0:3], in0=PS[2][:, 0:3],
                    scalar1=bias[:, oc:oc + 1]), deps=[("tensor", pe)])
                p.sop("vector", lambda e, oc=oc, ck=ck: e.tensor_scalar_mul(
                    out=xbuf[:, oc, 0:3], in0=xbuf[:, oc, 0:3],
                    scalar1=hflv[:, ck:ck + 1]))
            for s in range(2):
                t0 = ck * 1024 + s * 512
                ln_tile(xo, t0, 512)
                for oc in range(32):
                    bank = PS[2 + (oc % 2)]
                    vdn, scn, gdn = p.c["vector"], p.c["scalar"], p.c["gpsimd"]
                    for kc in range(8):
                        p.op("tensor", lambda e, kc=kc, oc=oc, bank=bank: e.matmul(
                            bank[:, :], ipbf[:, kc, oc * 128:(oc + 1) * 128],
                            xn[:, kc, :], start=(kc == 0), stop=(kc == 7)),
                            deps=[("vector", vdn), ("scalar", scn)] if kc == 0 else ())
                    pe = p.c["tensor"]
                    if oc < 16:
                        p.op("vector", lambda e, oc=oc, bank=bank: e.tensor_scalar_add(
                            out=xbuf[:, oc, 3:515], in0=bank[:, :],
                            scalar1=bias[:, oc:oc + 1]), deps=[("tensor", pe)])
                        p.op("vector", lambda e, oc=oc: e.tensor_scalar_mul(
                            out=acc[:, :], in0=xbuf[:, oc, 0:512],
                            scalar1=cwv[:, oc, 0:1]))
                        for j in range(1, 4):
                            p.op("vector", lambda e, oc=oc, j=j: e.scalar_tensor_tensor(
                                out=acc[:, :], in0=xbuf[:, oc, j:j + 512],
                                scalar=cwv[:, oc, j:j + 1], in1=acc[:, :],
                                op0=ALU.mult, op1=ALU.add))
                        p.op("vector", lambda e, oc=oc: e.tensor_copy(
                            out=xbuf[:, oc, 0:3], in_=xbuf[:, oc, 512:515]))
                        vv = p.c["vector"]
                        p.op("scalar", lambda e, oc=oc: e.activation(
                            out=ubf[:, :], in_=acc[:, :], func=AF.Silu,
                            bias=cbv[:, oc:oc + 1]),
                            deps=[("vector", vv), ("gpsimd", gdn)])
                        sc = p.c["scalar"]
                        p.op("gpsimd", lambda e, oc=oc, t0=t0: e.dma_start(
                            out=u_d[oc, :, t0:t0 + 512], in_=ubf[:, :]),
                            deps=[("scalar", sc)], dma=True)
                    else:
                        p.op("scalar", lambda e, oc=oc, bank=bank: e.activation(
                            out=szbf[:, :], in_=bank[:, :], func=AF.Sigmoid,
                            bias=bias[:, oc:oc + 1]),
                            deps=[("tensor", pe), ("gpsimd", gdn)])
                        sc = p.c["scalar"]
                        p.op("gpsimd", lambda e, oc=oc, t0=t0: e.dma_start(
                            out=sz_d[oc - 16, :, t0:t0 + 512], in_=szbf[:, :]),
                            deps=[("scalar", sc)], dma=True)


def build_mpre():
    nc = bass.Bass("TRN2", target_bir_lowering=False)
    xo = nc.dram_tensor("xo", [D, CTOK], F32, kind="ExternalInput")
    xh = nc.dram_tensor("xh", [D, 6], F32, kind="ExternalInput")
    ipw = nc.dram_tensor("ipw", [D, 4096], F32, kind="ExternalInput")
    ipb2 = nc.dram_tensor("ipb2", [128, 32], F32, kind="ExternalInput")
    cw2 = nc.dram_tensor("cw2", [128, 16, 4], F32, kind="ExternalInput")
    cb2 = nc.dram_tensor("cb2", [128, 16], F32, kind="ExternalInput")
    hfl = nc.dram_tensor("hfl", [128, 2], F32, kind="ExternalInput")
    onc = nc.dram_tensor("onc", [128, 1], F32, kind="ExternalInput")
    onr = nc.dram_tensor("onr", [1, 128], F32, kind="ExternalInput")
    u_d = nc.dram_tensor("u_d", [16, 128, CTOK], BF16, kind="ExternalOutput")
    sz_d = nc.dram_tensor("sz_d", [16, 128, CTOK], BF16, kind="ExternalOutput")

    d = dict(xo=xo, xh=xh, ipw=ipw, ipb2=ipb2, cw2=cw2, cb2=cb2, hfl=hfl, onc=onc, onr=onr, u_d=u_d, sz_d=sz_d)
    with ExitStack() as ctx:
        psb = ctx.enter_context(nc.psum_tensor("psb", [128, 4096], F32))
        PS = [psb[:, i * 512:(i + 1) * 512] for i in range(8)] + [psb]
        p = Prog(nc)
        lnw = LNWork(nc, ctx, PS[0], PS[1])
        lnw.load(p, d["onc"], d["onr"])
        emit_mpre(nc, ctx, p, lnw, PS, d)
        p.finalize(ctx)
    return nc



def emit_mmid(nc, ctx, p, lnw, PS, d, tag="Q", ectx=None):
    u_d = d["u_d"]; dtw = d["dtw"]; bw = d["bw"]; cw = d["cw"]; dtb2 = d["dtb2"]; al2 = d["al2"]; dec_d = d["dec_d"]; inc_d = d["inc_d"]; c_d = d["c_d"]; e_d = d["e_d"]
    with (ExitStack() if ectx is None else nullcontext(ectx)) as ectx:
        sb = lambda n, sh, dt_: ectx.enter_context(nc.sbuf_tensor(n + tag, sh, dt_))
        ubf = sb("ubf", [128, 16, CTOK], BF16)
        wbf = sb("wbf", [128, 16, 2048], BF16)
        wstage = sb("wstage", [128, 2048], F32)
        dtb = sb("dtb", [128, 16], F32)
        nA = sb("nA", [128, 16], F32)
        t0b = sb("t0b", [128, 512], F32)
        ab = sb("ab", [128, 512], F32)
        e1 = sb("e1", [128, 512], F32)
        lab = sb("lab", [128, 512], F32)
        rl = sb("rl", [128, 512], F32)
        sp4 = sb("sp4", [128, 512], F32)
        dcy = sb("dcy", [128, 512], F32)
        dch = sb("dch", [128, 512], F32)
        om = sb("om", [128, 512], F32)
        bb = sb("bb", [128, 512], F32)
        cbf = sb("cbf", [128, 512], BF16)
        eacc = sb("eacc", [128, 16, 2], F32)
        etmp = sb("etmp", [128, 1], F32)
        ee = sb("ee", [128, 32], F32)

        p.op("sync", lambda e: e.dma_start(out=dtb[:], in_=dtb2[:]))
        p.op("sync", lambda e: e.dma_start(out=nA[:], in_=al2[:]))
        p.op("scalar", lambda e: e.activation(out=nA[:], in_=nA[:], func=AF.Exp),
             deps=("sync",))
        p.op("vector", lambda e: e.tensor_scalar_mul(out=nA[:], in0=nA[:], scalar1=-1.0),
             deps=("scalar",))
        p.op("vector", lambda e: e.memset(eacc[:, :, :], 0.0))
        for oc in range(16):
            p.op("sync", lambda e, oc=oc: e.dma_start(out=ubf[:, oc, :], in_=u_d[oc]))

        def load_w(dram):
            for kc in range(16):
                p.op("sync", lambda e, kc=kc: e.dma_start(
                    out=wstage[:, :], in_=dram[kc * 128:(kc + 1) * 128, :]),
                    deps=("vector", "tensor"))
                p.op("vector", lambda e, kc=kc: e.tensor_copy(
                    out=wbf[:, kc, :], in_=wstage[:, :]), deps=("sync",))

        load_w(dtw)
        banksQ = Banks(p)
        for _bi in range(8):
            banksQ.free[_bi] = ("vector", p.c["vector"])
        w_ready = p.c["vector"]
        for oc in range(16):
            bbase = (oc % 2) * 4
            for kc in range(16):
                for t in range(4):
                    bi = bbase + t
                    deps = ()
                    if kc == 0:
                        deps = banksQ.wait(bi) + [("vector", w_ready)]
                    p.op("tensor", lambda e, kc=kc, oc=oc, t=t, bi=bi: e.matmul(
                        PS[bi][:, :], wbf[:, kc, oc * 128:(oc + 1) * 128],
                        ubf[:, kc, t * 512:(t + 1) * 512],
                        start=(kc == 0), stop=(kc == 15)),
                        deps=deps)
            pe = p.c["tensor"]
            for t in range(4):
                t0 = t * 512
                bi = bbase + t
                p.op("vector", lambda e, oc=oc, bi=bi: e.tensor_scalar_add(
                    out=t0b[:, :], in0=PS[bi][:, :], scalar1=dtb[:, oc:oc + 1]),
                    deps=[("tensor", pe)] if t == 0 else ())
                banksQ.evac(bi, "vector")
                vv = p.c["vector"]
                p.op("scalar", lambda e: e.activation(out=ab[:, :], in_=t0b[:, :],
                                                      func=AF.Abs),
                     deps=[("vector", vv)])
                p.op("scalar", lambda e: e.activation(out=ab[:, :], in_=ab[:, :],
                                                      func=AF.Exp, scale=-1.0))
                sc = p.c["scalar"]
                p.op("vector", lambda e: e.tensor_scalar_add(out=e1[:, :], in0=ab[:, :],
                                                             scalar1=1.0),
                     deps=[("scalar", sc)])
                vv = p.c["vector"]
                p.op("scalar", lambda e: e.activation(out=lab[:, :], in_=e1[:, :],
                                                      func=AF.Ln),
                     deps=[("vector", vv)])
                p.op("vector", lambda e: e.tensor_relu(out=rl[:, :], in_=t0b[:, :]))
                sc = p.c["scalar"]
                p.op("vector", lambda e: e.scalar_tensor_tensor(
                    out=sp4[:, :], in0=rl[:, :], scalar=1e-4, in1=lab[:, :],
                    op0=ALU.add, op1=ALU.add), deps=[("scalar", sc)])
                vv = p.c["vector"]
                p.op("scalar", lambda e, oc=oc: e.activation(
                    out=dcy[:, :], in_=sp4[:, :], func=AF.Exp,
                    scale=nA[:, oc:oc + 1]),
                    deps=[("vector", vv), ("gpsimd", p.c["gpsimd"])])
                sc = p.c["scalar"]
                p.op("gpsimd", lambda e, oc=oc, t0=t0: e.dma_start(
                    out=dec_d[oc, :, t0:t0 + 512], in_=dcy[:, :]),
                    deps=[("scalar", sc)], dma=True)
                p.sop("vector", lambda e: e.reduce_sum(out=etmp[:, :], in_=sp4[:, :],
                                                       axis=AX))
                p.sop("vector", lambda e, oc=oc, t=t: e.tensor_tensor(
                    out=eacc[:, oc, t // 2:t // 2 + 1], in0=eacc[:, oc, t // 2:t // 2 + 1],
                    in1=etmp[:, :], op=ALU.add))
        for oc in range(16):
            vv = p.c["vector"]
            p.op("scalar", lambda e, oc=oc: e.activation(
                out=ee[:, oc * 2:oc * 2 + 2], in_=eacc[:, oc, :], func=AF.Exp,
                scale=nA[:, oc:oc + 1]), deps=[("vector", vv)])
        sc = p.c["scalar"]
        p.op("gpsimd", lambda e: e.dma_start(out=e_d[:], in_=ee[:]),
             deps=[("scalar", sc)], dma=True)
        if d.get("ce") is not None:
            ce = d["ce"]
            ids_h = sb("idsq", [128, 128], F32)
            tev = sb("tevq", [2, 128], F32)
            p.op("sync", lambda e: e.dma_start(out=ids_h[:], in_=d["idf"][:]))
            for oc in range(16):
                p.op("tensor", lambda e, oc=oc: e.transpose(
                    PS[3][0:2, 0:128], ee[:, oc * 2:oc * 2 + 2], ids_h[:]),
                    deps=[("scalar", p.c["scalar"]), ("sync", p.c["sync"]),
                          ("gpsimd", p.c["gpsimd"])])
                pe = p.c["tensor"]
                p.op("scalar", lambda e: e.activation(out=tev[:], in_=PS[3][0:2, 0:128],
                                                      func=AF.Copy),
                     deps=[("tensor", pe), ("gpsimd", p.c["gpsimd"])])
                sc = p.c["scalar"]
                p.op("gpsimd", lambda e, oc=oc: e.dma_start(
                    out=ce[0:2, oc * 128:(oc + 1) * 128], in_=tev[:]),
                    deps=[("scalar", sc)], dma=True)

        load_w(bw)
        banksQ = Banks(p)
        for _bi in range(8):
            banksQ.free[_bi] = ("scalar", p.c["scalar"])
        w_ready = p.c["vector"]
        for oc in range(16):
            bbase = (oc % 2) * 4
            for kc in range(16):
                for t in range(4):
                    bi = bbase + t
                    deps = ()
                    if kc == 0:
                        deps = banksQ.wait(bi) + [("vector", w_ready)]
                    p.op("tensor", lambda e, kc=kc, oc=oc, t=t, bi=bi: e.matmul(
                        PS[bi][:, :], wbf[:, kc, oc * 128:(oc + 1) * 128],
                        ubf[:, kc, t * 512:(t + 1) * 512],
                        start=(kc == 0), stop=(kc == 15)),
                        deps=deps)
            pe = p.c["tensor"]
            for t in range(4):
                t0 = t * 512
                bi = bbase + t
                p.op("sync", lambda e, oc=oc, t0=t0: e.dma_start(
                    out=dch[:, :], in_=dec_d[oc, :, t0:t0 + 512]),
                    deps=[("vector", p.c["vector"])])
                p.op("scalar", lambda e, bi=bi: e.activation(
                    out=bb[:, :], in_=PS[bi][:, :], func=AF.Tanh),
                    deps=[("tensor", pe)] if t == 0 else ())
                banksQ.evac(bi, "scalar")
                ld = p.c["sync"]
                sc = p.c["scalar"]
                p.op("vector", lambda e: e.tensor_scalar(
                    out=om[:, :], in0=dch[:, :], scalar1=-1.0, scalar2=1.0,
                    op0=ALU.mult, op1=ALU.add),
                    deps=[("sync", ld), ("gpsimd", p.c["gpsimd"])])
                p.op("vector", lambda e: e.tensor_tensor(out=om[:, :], in0=om[:, :],
                                                         in1=bb[:, :], op=ALU.mult),
                     deps=[("scalar", sc)])
                p.op("vector", lambda e, oc=oc, t0=t0: e.tensor_tensor(
                    out=om[:, :], in0=om[:, :], in1=ubf[:, oc, t0:t0 + 512],
                    op=ALU.mult), deps=[("gpsimd", p.c["gpsimd"])])
                vv = p.c["vector"]
                p.op("gpsimd", lambda e, oc=oc, t0=t0: e.dma_start(
                    out=inc_d[oc, :, t0:t0 + 512], in_=om[:, :]),
                    deps=[("vector", vv)], dma=True)

        inc_done = p.c["gpsimd"]
        load_w(cw)
        banksQ = Banks(p)
        for _bi in range(8):
            banksQ.free[_bi] = ("vector", p.c["vector"])
        w_ready = p.c["vector"]
        for oc in range(16):
            bbase = (oc % 2) * 4
            for kc in range(16):
                for t in range(4):
                    bi = bbase + t
                    deps = ()
                    if kc == 0:
                        deps = banksQ.wait(bi) + [("vector", w_ready)]
                    p.op("tensor", lambda e, kc=kc, oc=oc, t=t, bi=bi: e.matmul(
                        PS[bi][:, :], wbf[:, kc, oc * 128:(oc + 1) * 128],
                        ubf[:, kc, t * 512:(t + 1) * 512],
                        start=(kc == 0), stop=(kc == 15)),
                        deps=deps)
            pe = p.c["tensor"]
            for t in range(4):
                t0 = t * 512
                bi = bbase + t
                p.op("scalar", lambda e, bi=bi: e.activation(
                    out=cbf[:, :], in_=PS[bi][:, :], func=AF.Tanh),
                    deps=[("tensor", pe), ("gpsimd", p.c["gpsimd"])] if t == 0 else [("gpsimd", p.c["gpsimd"])])
                banksQ.evac(bi, "scalar")
                sc = p.c["scalar"]
                p.op("gpsimd", lambda e, oc=oc, t0=t0: e.dma_start(
                    out=c_d[oc, :, t0:t0 + 512], in_=cbf[:, :]),
                    deps=[("scalar", sc)], dma=True)
    return {"inc_done": inc_done}


def build_mmid():
    nc = bass.Bass("TRN2", target_bir_lowering=False)
    u_d = nc.dram_tensor("u_d", [16, 128, CTOK], BF16, kind="ExternalInput")
    dtw = nc.dram_tensor("dtw", [2048, 2048], F32, kind="ExternalInput")
    bw = nc.dram_tensor("bw", [2048, 2048], F32, kind="ExternalInput")
    cw = nc.dram_tensor("cw", [2048, 2048], F32, kind="ExternalInput")
    dtb2 = nc.dram_tensor("dtb2", [128, 16], F32, kind="ExternalInput")
    al2 = nc.dram_tensor("al2", [128, 16], F32, kind="ExternalInput")
    dec_d = nc.dram_tensor("dec_d", [16, 128, CTOK], F32, kind="ExternalOutput")
    inc_d = nc.dram_tensor("inc_d", [16, 128, CTOK], F32, kind="ExternalOutput")
    c_d = nc.dram_tensor("c_d", [16, 128, CTOK], BF16, kind="ExternalOutput")
    e_d = nc.dram_tensor("e_d", [128, 32], F32, kind="ExternalOutput")

    d = dict(u_d=u_d, dtw=dtw, bw=bw, cw=cw, dtb2=dtb2, al2=al2, dec_d=dec_d, inc_d=inc_d, c_d=c_d, e_d=e_d)
    with ExitStack() as ctx:
        psb = ctx.enter_context(nc.psum_tensor("psb", [128, 4096], F32))
        PS = [psb[:, i * 512:(i + 1) * 512] for i in range(8)] + [psb]
        p = Prog(nc)
        lnw = LNWork(nc, ctx, PS[0], PS[1])
        emit_mmid(nc, ctx, p, lnw, PS, d)
        p.finalize(ctx)
    return nc



def emit_scan(nc, ctx, p, lnw, PS, d, tag="S", ready=None, ectx=None):
    dec_d = d["dec_d"]; inc_d = d["inc_d"]; onc = d["onc"]; st_d = d["st_d"]; f_d = d["f_d"]
    with (ExitStack() if ectx is None else nullcontext(ectx)) as ectx:
        sb = lambda n, sh, dt_: ectx.enter_context(nc.sbuf_tensor(n + tag, sh, dt_))
        dbuf = [sb(f"dbuf{i}", [128, 1024], F32) for i in range(2)]
        ibuf = [sb(f"ibuf{i}", [128, 1024], F32) for i in range(2)]
        obuf = [sb(f"obuf{i}", [128, 1024], F32) for i in range(2)]
        zc = sb("zc", [128, 1], F32)
        ff = sb("ff", [128, 32], F32)

        p.op("vector", lambda e: e.memset(zc[:], 0.0))
        scan_rd = {}
        store_g = {}
        v0 = p.c["vector"]
        for oc in range(16):
            for ck in range(2):
                sl = (oc * 2 + ck) % 2
                c0 = ck * 1024
                rdeps = [("vector", scan_rd.get(sl, v0))]
                if ready is not None:
                    rdeps.append(("gpsimd", ready))
                p.op("sync", lambda e, oc=oc, sl=sl, c0=c0: e.dma_start(
                    out=dbuf[sl][:, :], in_=dec_d[oc, :, c0:c0 + 1024]), deps=rdeps)
                p.op("sync", lambda e, oc=oc, sl=sl, c0=c0: e.dma_start(
                    out=ibuf[sl][:, :], in_=inc_d[oc, :, c0:c0 + 1024]))
                ld = p.c["sync"]
                sdeps = [("sync", ld)]
                if sl in store_g:
                    sdeps.append(("gpsimd", store_g[sl]))
                p.sop("vector", lambda e, sl=sl: e.tensor_tensor_scan(
                    out=obuf[sl][:, :], data0=dbuf[sl][:, :],
                    data1=ibuf[sl][:, :], initial=zc[:],
                    op0=ALU.mult, op1=ALU.add),
                    deps=sdeps)
                p.sop("vector", lambda e, sl=sl, oc=oc, ck=ck: e.tensor_copy(
                    out=ff[:, oc * 2 + ck:oc * 2 + ck + 1],
                    in_=obuf[sl][:, 1023:1024]))
                scan_rd[sl] = p.c["vector"]
                vv = p.c["vector"]
                p.op("gpsimd", lambda e, oc=oc, sl=sl, c0=c0: e.dma_start(
                    out=st_d[oc, :, c0:c0 + 1024], in_=obuf[sl][:, :]),
                    deps=[("vector", vv)], dma=True)
                store_g[sl] = p.c["gpsimd"]
        vv = p.c["vector"]
        p.op("gpsimd", lambda e: e.dma_start(out=f_d[:], in_=ff[:]),
             deps=[("vector", vv)], dma=True)
        if d.get("ce") is not None:
            ce = d["ce"]
            ids_h = sb("idss", [128, 128], F32)
            tev = sb("tevs", [2, 128], F32)
            p.op("sync", lambda e: e.dma_start(out=ids_h[:], in_=d["idf"][:]))
            for oc in range(16):
                p.op("tensor", lambda e, oc=oc: e.transpose(
                    PS[3][0:2, 0:128], ff[:, oc * 2:oc * 2 + 2], ids_h[:]),
                    deps=[("vector", p.c["vector"]), ("sync", p.c["sync"]),
                          ("scalar", p.c["scalar"]), ("gpsimd", p.c["gpsimd"])])
                pe = p.c["tensor"]
                p.op("scalar", lambda e: e.activation(out=tev[:], in_=PS[3][0:2, 0:128],
                                                      func=AF.Copy),
                     deps=[("tensor", pe), ("gpsimd", p.c["gpsimd"])])
                sc = p.c["scalar"]
                p.op("gpsimd", lambda e, oc=oc: e.dma_start(
                    out=ce[2:4, oc * 128:(oc + 1) * 128], in_=tev[:]),
                    deps=[("scalar", sc)], dma=True)


def build_scan():
    nc = bass.Bass("TRN2", target_bir_lowering=False)
    dec_d = nc.dram_tensor("dec_d", [16, 128, CTOK], F32, kind="ExternalInput")
    inc_d = nc.dram_tensor("inc_d", [16, 128, CTOK], F32, kind="ExternalInput")
    onc = nc.dram_tensor("onc", [128, 1], F32, kind="ExternalInput")
    st_d = nc.dram_tensor("st_d", [16, 128, CTOK], F32, kind="ExternalOutput")
    f_d = nc.dram_tensor("f_d", [128, 32], F32, kind="ExternalOutput")

    d = dict(dec_d=dec_d, inc_d=inc_d, onc=onc, st_d=st_d, f_d=f_d)
    with ExitStack() as ctx:
        psb = ctx.enter_context(nc.psum_tensor("psb", [128, 4096], F32))
        PS = [psb[:, i * 512:(i + 1) * 512] for i in range(8)] + [psb]
        p = Prog(nc)
        lnw = LNWork(nc, ctx, PS[0], PS[1])
        emit_scan(nc, ctx, p, lnw, PS, d)
        p.finalize(ctx)
    return nc



def emit_mpost(nc, ctx, p, lnw, PS, d, tag="R"):
    st_d = d["st_d"]; dec_d = d["dec_d"]; c_d = d["c_d"]; u_d = d["u_d"]; sz_d = d["sz_d"]; gef = d["gef"]; selq = d["selq"]; ow = d["ow"]; ob2 = d["ob2"]; d2 = d["d2"]; xo = d["xo"]; onc = d["onc"]; onr = d["onr"]; x3 = d["x3"]; y_dbg = d.get("y_dbg"); sin_dbg = d.get("sin_dbg"); st_dbg = d.get("st_dbg")
    with ExitStack() as ectx:
        sb = lambda n, sh, dt_: ectx.enter_context(nc.sbuf_tensor(n + tag, sh, dt_))
        owbf = sb("owbf", [128, 16, D], BF16)
        wstage = sb("wstage", [128, D], F32)
        ybf = sb("ybf", [128, 2, 16, 1024], BF16)
        stb = sb("stb", [128, 1024], F32)
        pb = sb("pb", [128, 1024], F32)
        dcb = sb("dcb", [128, 1024], F32)
        cb = sb("cb", [128, 1024], BF16)
        ub = sb("ub", [128, 1024], BF16)
        szb = sb("szb", [128, 1024], BF16)
        yv = sb("yv", [128, 1024], F32)
        gefs = sb("gefs", [128, 16, 8], F32)
        sel = sb("sel", [128, 8], F32)
        sinq = sb("sinq", [128, 16, 4], F32)
        sino = sb("sino", [128, 16, 2], F32)
        tq = sb("tq", [128, 1], F32)
        dv = sb("dv", [128, 16], F32)
        obv = sb("obv", [128, 8], F32)
        zc = sb("zc", [128, 1], F32)
        onec = sb("onec", [128, 1], F32)
        zb = sb("zb", [128, 1024], F32)
        xt = sb("xt", [128, 8, 512], F32)
        mix2 = sb("mix2", [128, 2, 8, 512], F32)

        p.op("sync", lambda e: e.dma_start(out=gefs[:], in_=gef[:]))
        p.op("sync", lambda e: e.dma_start(out=sel[:], in_=selq[:]))
        p.op("sync", lambda e: e.dma_start(out=dv[:], in_=d2[:]))
        p.op("sync", lambda e: e.dma_start(out=obv[:], in_=ob2[:]))
        p.op("vector", lambda e: e.memset(zc[:], 0.0))
        p.op("vector", lambda e: e.memset(onec[:], 1.0))
        p.op("vector", lambda e: e.memset(zb[:], 0.0))
        for kc in range(16):
            p.op("sync", lambda e, kc=kc: e.dma_start(
                out=wstage[:, :], in_=ow[kc * 128:(kc + 1) * 128, :]),
                deps=("vector",))
            p.op("vector", lambda e, kc=kc: e.tensor_copy(
                out=owbf[:, kc, :], in_=wstage[:, :]), deps=("sync",))

        for oc in range(16):
            p.sop("vector", lambda e, oc=oc: e.memset(sinq[:, oc, 0:1], 0.0),
                  deps=("sync",))
            p.sop("vector", lambda e, oc=oc: e.tensor_copy(
                out=sinq[:, oc, 1:2], in_=gefs[:, oc, 2:3]))
            p.sop("vector", lambda e, oc=oc: e.scalar_tensor_tensor(
                out=sinq[:, oc, 2:3], in0=gefs[:, oc, 4:5],
                scalar=sinq[:, oc, 1:2], in1=gefs[:, oc, 6:7],
                op0=ALU.mult, op1=ALU.add))
            p.sop("vector", lambda e, oc=oc: e.scalar_tensor_tensor(
                out=sinq[:, oc, 3:4], in0=gefs[:, oc, 5:6],
                scalar=sinq[:, oc, 2:3], in1=gefs[:, oc, 7:8],
                op0=ALU.mult, op1=ALU.add))
            for ck in range(2):
                p.sop("vector", lambda e, oc=oc, ck=ck: e.tensor_scalar_mul(
                    out=sino[:, oc, ck:ck + 1], in0=sinq[:, oc, 0:1],
                    scalar1=sel[:, ck * 4:ck * 4 + 1]))
                for q in range(1, 4):
                    p.sop("vector", lambda e, oc=oc, ck=ck, q=q: e.tensor_scalar(
                        out=tq[:, :], in0=sinq[:, oc, q:q + 1],
                        scalar1=sel[:, ck * 4 + q:ck * 4 + q + 1], scalar2=None,
                        op0=ALU.mult))
                    p.sop("vector", lambda e, oc=oc, ck=ck: e.tensor_tensor(
                        out=sino[:, oc, ck:ck + 1], in0=sino[:, oc, ck:ck + 1],
                        in1=tq[:, :], op=ALU.add))

        if DBG_MPOST:
            vv1 = p.c["vector"]
            p.op("gpsimd", lambda e: e.dma_start(
                out=sin_dbg[:], in_=sino.rearrange("p a b -> p (a b)")),
                deps=[("vector", vv1)], dma=True)
        banksR = Banks(p)
        for _bi in range(8):
            banksR.free[_bi] = ("vector", p.c["vector"])
        ybf_rd = {}
        for ck in range(2):
            c0 = ck * 1024
            ysl = ck % 2
            for oc in range(16):
                gd = p.c["gpsimd"]
                p.op("sync", lambda e, oc=oc, c0=c0: e.dma_start(
                    out=stb[:, :], in_=st_d[oc, :, c0:c0 + 1024]),
                    deps=[("vector", p.c["vector"])])
                p.op("sync", lambda e, oc=oc, c0=c0: e.dma_start(
                    out=dcb[:, :], in_=dec_d[oc, :, c0:c0 + 1024]))
                p.op("sync", lambda e, oc=oc, c0=c0: e.dma_start(
                    out=cb[:, :], in_=c_d[oc, :, c0:c0 + 1024]))
                p.op("sync", lambda e, oc=oc, c0=c0: e.dma_start(
                    out=ub[:, :], in_=u_d[oc, :, c0:c0 + 1024]))
                p.op("sync", lambda e, oc=oc, c0=c0: e.dma_start(
                    out=szb[:, :], in_=sz_d[oc, :, c0:c0 + 1024]))
                ld = p.c["sync"]
                p.sop("vector", lambda e: e.tensor_tensor_scan(
                    out=pb[:, :], data0=dcb[:, :], data1=zb[:, :],
                    initial=onec[:], op0=ALU.mult, op1=ALU.add),
                    deps=[("sync", ld)])
                p.sop("vector", lambda e, oc=oc, ck=ck: e.scalar_tensor_tensor(
                    out=stb[:, :], in0=pb[:, :], scalar=sino[:, oc, ck:ck + 1],
                    in1=stb[:, :], op0=ALU.mult, op1=ALU.add))
                p.op("vector", lambda e: e.tensor_tensor(out=yv[:, :], in0=stb[:, :],
                                                         in1=cb[:, :], op=ALU.mult))
                p.sop("vector", lambda e, oc=oc: e.scalar_tensor_tensor(
                    out=yv[:, :], in0=ub[:, :], scalar=dv[:, oc:oc + 1], in1=yv[:, :],
                    op0=ALU.mult, op1=ALU.add))
                p.op("vector", lambda e, oc=oc, ysl=ysl: e.tensor_tensor(
                    out=ybf[:, ysl, oc, :], in0=yv[:, :], in1=szb[:, :], op=ALU.mult),
                    deps=[("gpsimd", gd)] + ([("tensor", ybf_rd[ysl])] if (oc == 0 and ysl in ybf_rd) else []))
                if DBG_MPOST:
                    vv1 = p.c["vector"]
                    p.op("gpsimd", lambda e, oc=oc, ck=ck: e.dma_start(
                        out=y_dbg[ck, oc], in_=ybf[:, oc, :]),
                        deps=[("vector", vv1)], dma=True)
                    p.op("gpsimd", lambda e, oc=oc, ck=ck: e.dma_start(
                        out=st_dbg[ck, oc], in_=stb[:, :]),
                        deps=[("vector", vv1)], dma=True)
            vdn = p.c["vector"]
            for oc2 in range(8):
                bb = 2 + (oc2 % 3) * 2
                for kc in range(16):
                    for t in range(2):
                        bi = bb + t
                        deps = ()
                        if kc == 0:
                            deps = banksR.wait(bi) + [("vector", vdn)]
                        p.op("tensor", lambda e, kc=kc, oc2=oc2, t=t, ysl=ysl, bi=bi: e.matmul(
                            PS[bi][:, :], owbf[:, kc, oc2 * 128:(oc2 + 1) * 128],
                            ybf[:, ysl, kc, t * 512:(t + 1) * 512],
                            start=(kc == 0), stop=(kc == 15)),
                            deps=deps)
                pe = p.c["tensor"]
                for t in range(2):
                    bi = bb + t
                    p.op("vector", lambda e, oc2=oc2, t=t, bi=bi: e.tensor_scalar_add(
                        out=mix2[:, t, oc2, :], in0=PS[bi][:, :],
                        scalar1=obv[:, oc2:oc2 + 1]),
                        deps=[("tensor", pe)] if t == 0 else ())
                    banksR.evac(bi, "vector")
            for t in range(2):
                t0 = c0 + t * 512
                for c in range(8):
                    p.op("sync", lambda e, c=c, t0=t0: e.dma_start(
                        out=xt[:, c, :], in_=xo[c * 128:(c + 1) * 128, t0:t0 + 512]),
                        deps=("vector", "gpsimd") if c == 0 else ())
                emit_ln(p, lnw, 512, src=lambda c, t=t: mix2[:, t, c, :],
                        resid=lambda c: xt[:, c, :])
                vv = p.c["vector"]
                for c in range(8):
                    p.op("gpsimd", lambda e, c=c, t0=t0: e.dma_start(
                        out=x3[c * 128:(c + 1) * 128, t0:t0 + 512], in_=xt[:, c, :]),
                        deps=[("vector", vv)], dma=True)
            ybf_rd[ysl] = p.c["tensor"]


def build_mpost():
    nc = bass.Bass("TRN2", target_bir_lowering=False)
    st_d = nc.dram_tensor("st_d", [16, 128, CTOK], F32, kind="ExternalInput")
    dec_d = nc.dram_tensor("dec_d", [16, 128, CTOK], F32, kind="ExternalInput")
    c_d = nc.dram_tensor("c_d", [16, 128, CTOK], BF16, kind="ExternalInput")
    u_d = nc.dram_tensor("u_d", [16, 128, CTOK], BF16, kind="ExternalInput")
    sz_d = nc.dram_tensor("sz_d", [16, 128, CTOK], BF16, kind="ExternalInput")
    gef = nc.dram_tensor("gef", [128, 16, 8], F32, kind="ExternalInput")
    selq = nc.dram_tensor("selq", [128, 8], F32, kind="ExternalInput")
    ow = nc.dram_tensor("ow", [2048, D], F32, kind="ExternalInput")
    ob2 = nc.dram_tensor("ob2", [128, 8], F32, kind="ExternalInput")
    d2 = nc.dram_tensor("d2", [128, 16], F32, kind="ExternalInput")
    xo = nc.dram_tensor("xo", [D, CTOK], F32, kind="ExternalInput")
    onc = nc.dram_tensor("onc", [128, 1], F32, kind="ExternalInput")
    onr = nc.dram_tensor("onr", [1, 128], F32, kind="ExternalInput")
    x3 = nc.dram_tensor("x3", [D, CTOK], F32, kind="ExternalOutput")
    if DBG_MPOST:
        y_dbg = nc.dram_tensor("y_dbg", [2, 16, 128, 1024], BF16, kind="ExternalOutput")
        sin_dbg = nc.dram_tensor("sin_dbg", [128, 32], F32, kind="ExternalOutput")
        st_dbg = nc.dram_tensor("st_dbg", [2, 16, 128, 1024], F32, kind="ExternalOutput")

    d = dict(st_d=st_d, dec_d=dec_d, c_d=c_d, u_d=u_d, sz_d=sz_d, gef=gef, selq=selq, ow=ow, ob2=ob2, d2=d2, xo=xo, onc=onc, onr=onr, x3=x3)
    if DBG_MPOST:
        d.update(y_dbg=y_dbg, sin_dbg=sin_dbg, st_dbg=st_dbg)
    with ExitStack() as ctx:
        psb = ctx.enter_context(nc.psum_tensor("psb", [128, 4096], F32))
        PS = [psb[:, i * 512:(i + 1) * 512] for i in range(8)] + [psb]
        p = Prog(nc)
        lnw = LNWork(nc, ctx, PS[0], PS[1])
        lnw.load(p, d["onc"], d["onr"])
        emit_mpost(nc, ctx, p, lnw, PS, d)
        p.finalize(ctx)
    return nc


def _unshard_cm(per_core, nch):
    out = np.empty((B, T, nch * 128), dtype=np.asarray(per_core[0]).dtype)
    for c in range(NC):
        b, qr = core_quarters(c)
        arr = np.asarray(per_core[c])
        off = 0
        for (a, bb) in qr:
            w = bb - a
            out[b, a:bb, :] = arr[:, :, off:off + w].reshape(nch * 128, w).T
            off += w
    return out


def run_mamba(inputs, x2full, dbg=False):
    xfm = _fm(x2full)
    onc = np.ones((128, 1), dtype=np.float32)
    onr = np.ones((1, 128), dtype=np.float32)
    ipw = np.ascontiguousarray(np.asarray(inputs["L1_attn_in_proj_w"], np.float32).T)
    ipb2 = chunkcol(inputs["L1_attn_in_proj_b"])
    cw2 = np.ascontiguousarray(np.asarray(inputs["L1_attn_conv_w"], np.float32)[:, 0, :]
                               .reshape(16, 128, 4).transpose(1, 0, 2))
    cb2 = chunkcol(inputs["L1_attn_conv_b"])
    dtw = np.ascontiguousarray(np.asarray(inputs["L1_attn_dt_w"], np.float32).T)
    bw = np.ascontiguousarray(np.asarray(inputs["L1_attn_b_w"], np.float32).T)
    cwt = np.ascontiguousarray(np.asarray(inputs["L1_attn_c_w"], np.float32).T)
    dtb2 = chunkcol(inputs["L1_attn_dt_b"])
    al2 = chunkcol(inputs["L1_attn_A_log"])
    d2 = chunkcol(inputs["L1_attn_D_param"])
    ow = np.ascontiguousarray(np.asarray(inputs["L1_attn_out_w"], np.float32).T)
    ob2 = chunkcol(inputs["L1_attn_out_b"])

    in_maps = []
    for c in range(NC):
        b, qr = core_quarters(c)
        xh = np.zeros((D, 6), dtype=np.float32)
        hfl = np.zeros((128, 2), dtype=np.float32)
        for ck, (a, bb) in enumerate(qr):
            if a > 0:
                xh[:, ck * 3:(ck + 1) * 3] = xfm[b][:, a - 3:a]
                hfl[:, ck] = 1.0
            else:
                xh[:, ck * 3:(ck + 1) * 3] = xfm[b][:, a:a + 3]
        in_maps.append({"xo": _own_cols(xfm, c), "xh": xh, "ipw": ipw,
                        "ipb2": ipb2, "cw2": cw2, "cb2": cb2, "hfl": hfl,
                        "onc": onc, "onr": onr})
    r1 = run_bass_kernel_spmd(build_mpre(), in_maps, core_ids=list(range(NC))).results

    in_maps = [{"u_d": np.asarray(r1[c]["u_d"]), "dtw": dtw, "bw": bw, "cw": cwt,
                "dtb2": dtb2, "al2": al2} for c in range(NC)]
    r2 = run_bass_kernel_spmd(build_mmid(), in_maps, core_ids=list(range(NC))).results

    in_maps = [{"dec_d": np.asarray(r2[c]["dec_d"]), "inc_d": np.asarray(r2[c]["inc_d"]),
                "onc": onc} for c in range(NC)]
    r3 = run_bass_kernel_spmd(build_scan(), in_maps, core_ids=list(range(NC))).results

    gefs, selqs = [], []
    for c in range(NC):
        b = c // 2
        ca, cb = 2 * b, 2 * b + 1
        G = np.zeros((128, 16, 8), dtype=np.float32)
        ea_ = np.asarray(r2[ca]["e_d"]); eb_ = np.asarray(r2[cb]["e_d"])
        fa_ = np.asarray(r3[ca]["f_d"]); fb_ = np.asarray(r3[cb]["f_d"])
        for oc in range(16):
            G[:, oc, 0] = ea_[:, oc * 2 + 0]
            G[:, oc, 1] = ea_[:, oc * 2 + 1]
            G[:, oc, 2] = fa_[:, oc * 2 + 0]
            G[:, oc, 3] = fa_[:, oc * 2 + 1]
            G[:, oc, 4] = eb_[:, oc * 2 + 0]
            G[:, oc, 5] = eb_[:, oc * 2 + 1]
            G[:, oc, 6] = fb_[:, oc * 2 + 0]
            G[:, oc, 7] = fb_[:, oc * 2 + 1]
        gefs.append(G)
        sq = np.zeros((128, 8), dtype=np.float32)
        quarters = (0, 3) if c % 2 == 0 else (1, 2)
        sq[:, 0 * 4 + quarters[0]] = 1.0
        sq[:, 1 * 4 + quarters[1]] = 1.0
        selqs.append(sq)

    in_maps = [{"st_d": np.asarray(r3[c]["st_d"]), "dec_d": np.asarray(r2[c]["dec_d"]),
                "c_d": np.asarray(r2[c]["c_d"]), "u_d": np.asarray(r1[c]["u_d"]),
                "sz_d": np.asarray(r1[c]["sz_d"]), "gef": gefs[c],
                "selq": selqs[c], "ow": ow, "ob2": ob2, "d2": d2,
                "xo": _own_cols(xfm, c), "onc": onc, "onr": onr}
               for c in range(NC)]
    r4 = run_bass_kernel_spmd(build_mpost(), in_maps, core_ids=list(range(NC))).results
    x3 = _scatter_back([np.asarray(r["x3"]) for r in r4])
    if dbg:
        return x3, {"r1": r1, "r2": r2, "r3": r3, "r4": r4}
    return x3



def emit_tern1(nc, ctx, p, lnw, PS, d, tag="T"):
    xo = d["xo"]; w1t = d["w1t"]; w2t = d["w2t"]; s1b = d["s1b"]; s2b = d["s2b"]; onc = d["onc"]; onr = d["onr"]; h0_d = d["h0_d"]; t1_d = d["t1_d"]
    with ExitStack() as ectx:
        sb = lambda n, sh, dt_: ectx.enter_context(nc.sbuf_tensor(n + tag, sh, dt_))
        xt = sb("xt", [128, 8, 512], F32)
        xn = sb("xn", [128, 8, CTOK], BF16)
        wbf = sb("wbf", [128, 8, 4096], BF16)
        wstage = sb("wstage", [128, 4096], F32)
        sv1 = sb("sv1", [128, 32], F32)
        sv2 = sb("sv2", [128, 32], F32)
        t1b = sb("t1b", [128, 4, 512], BF16)
        b2 = sb("b2", [128, 512], F32)
        hbf = sb("hbf", [128, 512], BF16)

        p.op("sync", lambda e: e.dma_start(out=sv1[:], in_=s1b[:]))
        p.op("sync", lambda e: e.dma_start(out=sv2[:], in_=s2b[:]))
        for s in range(4):
            t0 = s * 512
            for c in range(8):
                p.op("sync", lambda e, c=c, t0=t0: e.dma_start(
                    out=xt[:, c, :], in_=xo[c * 128:(c + 1) * 128, t0:t0 + 512]),
                    deps=("scalar", "tensor", "vector") if c == 0 else ())
            emit_ln(p, lnw, 512, src=lambda c: xt[:, c, :],
                    dst=lambda c, t0=t0: xn[:, c, t0:t0 + 512])

        def load_w(dram):
            for kc in range(8):
                p.op("sync", lambda e, kc=kc: e.dma_start(
                    out=wstage[:, :], in_=dram[kc * 128:(kc + 1) * 128, :]),
                    deps=("vector", "tensor"))
                p.op("vector", lambda e, kc=kc: e.tensor_copy(
                    out=wbf[:, kc, :], in_=wstage[:, :]), deps=("sync",))

        load_w(w1t)
        banksT = Banks(p)
        for _bi in range(8):
            banksT.free[_bi] = ("vector", p.c["vector"])
        w_ready = p.c["vector"]
        for oc in range(32):
            bbase = (oc % 2) * 4
            for kc in range(8):
                for t in range(4):
                    bi = bbase + t
                    deps = ()
                    if kc == 0:
                        deps = banksT.wait(bi) + [("vector", w_ready)]
                    p.op("tensor", lambda e, kc=kc, oc=oc, t=t, bi=bi: e.matmul(
                        PS[bi][:, :], wbf[:, kc, oc * 128:(oc + 1) * 128],
                        xn[:, kc, t * 512:(t + 1) * 512],
                        start=(kc == 0), stop=(kc == 7)),
                        deps=deps)
            pe = p.c["tensor"]
            for t in range(4):
                t0 = t * 512
                bi = bbase + t
                gdn = p.c["gpsimd"]
                p.op("scalar", lambda e, oc=oc, bi=bi: e.activation(
                    out=hbf[:, :], in_=PS[bi][:, :], func=AF.Silu,
                    scale=sv1[:, oc:oc + 1]),
                    deps=[("tensor", pe), ("gpsimd", gdn)] if t == 0 else [("gpsimd", gdn)])
                banksT.evac(bi, "scalar")
                sc = p.c["scalar"]
                p.op("gpsimd", lambda e, oc=oc, t0=t0: e.dma_start(
                    out=t1_d[oc, :, t0:t0 + 512], in_=hbf[:, :]),
                    deps=[("scalar", sc)], dma=True)
        load_w(w2t)
        banksT = Banks(p)
        for _bi in range(8):
            banksT.free[_bi] = ("vector", p.c["vector"])
        w_ready = p.c["vector"]
        for oc in range(32):
            bbase = (oc % 2) * 4
            for t in range(4):
                p.op("sync", lambda e, oc=oc, t=t: e.dma_start(
                    out=t1b[:, t, :], in_=t1_d[oc, :, t * 512:(t + 1) * 512]),
                    deps=[("vector", p.c["vector"])] if t == 0 else ())
            ld = p.c["sync"]
            for kc in range(8):
                for t in range(4):
                    bi = bbase + t
                    deps = ()
                    if kc == 0:
                        deps = banksT.wait(bi) + [("vector", w_ready)]
                    p.op("tensor", lambda e, kc=kc, oc=oc, t=t, bi=bi: e.matmul(
                        PS[bi][:, :], wbf[:, kc, oc * 128:(oc + 1) * 128],
                        xn[:, kc, t * 512:(t + 1) * 512],
                        start=(kc == 0), stop=(kc == 7)),
                        deps=deps)
            pe = p.c["tensor"]
            for t in range(4):
                t0 = t * 512
                bi = bbase + t
                gdn = p.c["gpsimd"]
                p.op("vector", lambda e, oc=oc, bi=bi: e.tensor_scalar_mul(
                    out=b2[:, :], in0=PS[bi][:, :], scalar1=sv2[:, oc:oc + 1]),
                    deps=[("tensor", pe), ("sync", ld)] if t == 0 else ())
                banksT.evac(bi, "vector")
                p.op("vector", lambda e, oc=oc, t=t: e.tensor_tensor(
                    out=hbf[:, :], in0=t1b[:, t, :], in1=b2[:, :], op=ALU.mult),
                    deps=[("gpsimd", gdn)])
                vv = p.c["vector"]
                p.op("gpsimd", lambda e, oc=oc, t0=t0: e.dma_start(
                    out=h0_d[oc, :, t0:t0 + 512], in_=hbf[:, :]),
                    deps=[("vector", vv)], dma=True)


def build_tern1():
    nc = bass.Bass("TRN2", target_bir_lowering=False)
    xo = nc.dram_tensor("xo", [D, CTOK], F32, kind="ExternalInput")
    w1t = nc.dram_tensor("w1t", [D, 4096], F32, kind="ExternalInput")
    w2t = nc.dram_tensor("w2t", [D, 4096], F32, kind="ExternalInput")
    s1b = nc.dram_tensor("s1b", [128, 32], F32, kind="ExternalInput")
    s2b = nc.dram_tensor("s2b", [128, 32], F32, kind="ExternalInput")
    onc = nc.dram_tensor("onc", [128, 1], F32, kind="ExternalInput")
    onr = nc.dram_tensor("onr", [1, 128], F32, kind="ExternalInput")
    h0_d = nc.dram_tensor("h0_d", [32, 128, CTOK], BF16, kind="ExternalOutput")
    t1_d = nc.dram_tensor("t1_d", [32, 128, CTOK], BF16)

    d = dict(xo=xo, w1t=w1t, w2t=w2t, s1b=s1b, s2b=s2b, onc=onc, onr=onr, h0_d=h0_d, t1_d=t1_d)
    with ExitStack() as ctx:
        psb = ctx.enter_context(nc.psum_tensor("psb", [128, 4096], F32))
        PS = [psb[:, i * 512:(i + 1) * 512] for i in range(8)] + [psb]
        p = Prog(nc)
        lnw = LNWork(nc, ctx, PS[0], PS[1])
        lnw.load(p, d["onc"], d["onr"])
        emit_tern1(nc, ctx, p, lnw, PS, d)
        p.finalize(ctx)
    return nc


def emit_tern_mid(nc, ctx, p, lnw, PS, d, tag="U"):
    h_in = d["h_in"]; wt = d["wt"]; sb_i = d["sb_i"]; h_out = d["h_out"]
    with ExitStack() as ectx:
        sb = lambda n, sh, dt_: ectx.enter_context(nc.sbuf_tensor(n + tag, sh, dt_))
        hb = sb("hb", [128, 32, CTOK], BF16)
        woc = [sb(f"woc{i}", [128, 32, 128], BF16) for i in range(2)]
        wstage = sb("wstage", [128, 4096], F32)
        sv = sb("sv", [128, 32], F32)
        hbf = sb("hbf", [128, 4, 512], BF16)

        p.op("sync", lambda e: e.dma_start(out=sv[:], in_=sb_i[:]))
        for oc in range(32):
            p.op("sync", lambda e, oc=oc: e.dma_start(out=hb[:, oc, :], in_=h_in[oc]))
        banksU = Banks(p)
        for _bi in range(8):
            banksU.free[_bi] = ("vector", p.c["vector"])
        woc_rd = {}
        for oc in range(32):
            sl = oc % 2
            for kc in range(32):
                p.op("sync", lambda e, kc=kc, oc=oc: e.dma_start(
                    out=wstage[:, kc * 128:(kc + 1) * 128],
                    in_=wt[kc * 128:(kc + 1) * 128, oc * 128:(oc + 1) * 128]),
                    deps=[("vector", p.c["vector"])] if kc == 0 else ())
            ld = p.c["sync"]
            cdeps = [("sync", ld)]
            if sl in woc_rd:
                cdeps.append(("tensor", woc_rd[sl]))
            p.op("vector", lambda e, sl=sl: e.tensor_copy(
                out=woc[sl].rearrange("p a b -> p (a b)"),
                in_=wstage[:, :]), deps=cdeps)
            vdn = p.c["vector"]
            bbase = (oc % 2) * 4
            for kc in range(32):
                for t in range(4):
                    bi = bbase + t
                    deps = ()
                    if kc == 0:
                        deps = banksU.wait(bi) + [("vector", vdn)]
                    p.op("tensor", lambda e, kc=kc, oc=oc, t=t, sl=sl, bi=bi: e.matmul(
                        PS[bi][:, :], woc[sl][:, kc, :],
                        hb[:, kc, t * 512:(t + 1) * 512],
                        start=(kc == 0), stop=(kc == 31)),
                        deps=deps)
            pe = p.c["tensor"]
            gdn = p.c["gpsimd"]
            for t in range(4):
                bi = bbase + t
                p.op("scalar", lambda e, oc=oc, t=t, bi=bi: e.activation(
                    out=hbf[:, t, :], in_=PS[bi][:, :], func=AF.Gelu,
                    scale=sv[:, oc:oc + 1]),
                    deps=[("tensor", pe), ("gpsimd", gdn)] if t == 0 else ())
                banksU.evac(bi, "scalar")
                sc = p.c["scalar"]
                p.op("gpsimd", lambda e, oc=oc, t=t: e.dma_start(
                    out=h_out[oc, :, t * 512:(t + 1) * 512], in_=hbf[:, t, :]),
                    deps=[("scalar", sc)], dma=True)
            woc_rd[sl] = p.c["tensor"]


def build_tern_mid():
    nc = bass.Bass("TRN2", target_bir_lowering=False)
    h_in = nc.dram_tensor("h_in", [32, 128, CTOK], BF16, kind="ExternalInput")
    wt = nc.dram_tensor("wt", [4096, 4096], F32, kind="ExternalInput")
    sb_i = nc.dram_tensor("sb_i", [128, 32], F32, kind="ExternalInput")
    h_out = nc.dram_tensor("h_out", [32, 128, CTOK], BF16, kind="ExternalOutput")

    d = dict(h_in=h_in, wt=wt, sb_i=sb_i, h_out=h_out)
    with ExitStack() as ctx:
        psb = ctx.enter_context(nc.psum_tensor("psb", [128, 4096], F32))
        PS = [psb[:, i * 512:(i + 1) * 512] for i in range(8)] + [psb]
        p = Prog(nc)
        lnw = LNWork(nc, ctx, PS[0], PS[1])
        emit_tern_mid(nc, ctx, p, lnw, PS, d)
        p.finalize(ctx)
    return nc


def emit_tern4(nc, ctx, p, lnw, PS, d, tag="V"):
    h_in = d["h_in"]; w3t = d["w3t"]; s3b = d["s3b"]; xo = d["xo"]; onc = d["onc"]; onr = d["onr"]; x4 = d["x4"]
    with ExitStack() as ectx:
        sb = lambda n, sh, dt_: ectx.enter_context(nc.sbuf_tensor(n + tag, sh, dt_))
        w3bf = sb("w3bf", [128, 32, D], BF16)
        wstage = sb("wstage", [128, D], F32)
        hb = sb("hb", [128, 32, 512], BF16)
        sv = sb("sv", [128, 8], F32)
        xt = sb("xt", [128, 8, 512], F32)
        mix = sb("mix", [128, 8, 512], F32)

        p.op("sync", lambda e: e.dma_start(out=sv[:], in_=s3b[:]))
        for kc in range(32):
            p.op("sync", lambda e, kc=kc: e.dma_start(
                out=wstage[:, :], in_=w3t[kc * 128:(kc + 1) * 128, :]),
                deps=("vector",))
            p.op("vector", lambda e, kc=kc: e.tensor_copy(
                out=w3bf[:, kc, :], in_=wstage[:, :]), deps=("sync",))
        banksV = Banks(p)
        for _bi in range(8):
            banksV.free[_bi] = ("vector", p.c["vector"])
        w_ready = p.c["vector"]
        for s in range(4):
            t0 = s * 512
            for kc in range(32):
                p.op("sync", lambda e, kc=kc, t0=t0: e.dma_start(
                    out=hb[:, kc, :], in_=h_in[kc, :, t0:t0 + 512]),
                    deps=("tensor",) if kc == 0 else ())
            for c in range(8):
                p.op("sync", lambda e, c=c, t0=t0: e.dma_start(
                    out=xt[:, c, :], in_=xo[c * 128:(c + 1) * 128, t0:t0 + 512]),
                    deps=("vector", "gpsimd") if c == 0 else ())
            ld = p.c["sync"]
            for oc in range(8):
                bi = 2 + (oc % 4)
                bank = PS[bi]
                for kc in range(32):
                    p.op("tensor", lambda e, kc=kc, oc=oc, bank=bank: e.matmul(
                        bank[:, :], w3bf[:, kc, oc * 128:(oc + 1) * 128],
                        hb[:, kc, :], start=(kc == 0), stop=(kc == 31)),
                        deps=([("sync", ld)] + banksV.wait(bi) +
                              [("vector", w_ready)]) if kc == 0 else ())
                pe = p.c["tensor"]
                p.op("vector", lambda e, oc=oc, bank=bank: e.tensor_scalar_mul(
                    out=mix[:, oc, :], in0=bank[:, :], scalar1=sv[:, oc:oc + 1]),
                    deps=[("tensor", pe)])
                banksV.evac(bi, "vector")
            emit_ln(p, lnw, 512, src=lambda c: mix[:, c, :],
                    resid=lambda c: xt[:, c, :])
            vv = p.c["vector"]
            for c in range(8):
                p.op("gpsimd", lambda e, c=c, t0=t0: e.dma_start(
                    out=x4[c * 128:(c + 1) * 128, t0:t0 + 512], in_=xt[:, c, :]),
                    deps=[("vector", vv)], dma=True)


def build_tern4():
    nc = bass.Bass("TRN2", target_bir_lowering=False)
    h_in = nc.dram_tensor("h_in", [32, 128, CTOK], BF16, kind="ExternalInput")
    w3t = nc.dram_tensor("w3t", [4096, D], F32, kind="ExternalInput")
    s3b = nc.dram_tensor("s3b", [128, 8], F32, kind="ExternalInput")
    xo = nc.dram_tensor("xo", [D, CTOK], F32, kind="ExternalInput")
    onc = nc.dram_tensor("onc", [128, 1], F32, kind="ExternalInput")
    onr = nc.dram_tensor("onr", [1, 128], F32, kind="ExternalInput")
    x4 = nc.dram_tensor("x4", [D, CTOK], F32, kind="ExternalOutput")

    d = dict(h_in=h_in, w3t=w3t, s3b=s3b, xo=xo, onc=onc, onr=onr, x4=x4)
    with ExitStack() as ctx:
        psb = ctx.enter_context(nc.psum_tensor("psb", [128, 4096], F32))
        PS = [psb[:, i * 512:(i + 1) * 512] for i in range(8)] + [psb]
        p = Prog(nc)
        lnw = LNWork(nc, ctx, PS[0], PS[1])
        lnw.load(p, d["onc"], d["onr"])
        emit_tern4(nc, ctx, p, lnw, PS, d)
        p.finalize(ctx)
    return nc


def run_tern(inputs, x3full):
    xfm = _fm(x3full)
    onc = np.ones((128, 1), dtype=np.float32)
    onr = np.ones((1, 128), dtype=np.float32)
    w1t = np.ascontiguousarray(np.asarray(inputs["L1_ffn_w1_w"], np.float32).T)
    w2t = np.ascontiguousarray(np.asarray(inputs["L1_ffn_w2_w"], np.float32).T)
    h1t = np.ascontiguousarray(np.asarray(inputs["L1_ffn_h1_w"], np.float32).T)
    h2t = np.ascontiguousarray(np.asarray(inputs["L1_ffn_h2_w"], np.float32).T)
    w3t = np.ascontiguousarray(np.asarray(inputs["L1_ffn_w3_w"], np.float32).T)
    s1b = chunkcol(inputs["L1_ffn_w1_s"])
    s2b = chunkcol(inputs["L1_ffn_w2_s"])
    h1s = chunkcol(inputs["L1_ffn_h1_s"])
    h2s = chunkcol(inputs["L1_ffn_h2_s"])
    s3b = chunkcol(inputs["L1_ffn_w3_s"])

    in_maps = [{"xo": _own_cols(xfm, c), "w1t": w1t, "w2t": w2t,
                "s1b": s1b, "s2b": s2b, "onc": onc, "onr": onr}
               for c in range(NC)]
    r1 = run_bass_kernel_spmd(build_tern1(), in_maps, core_ids=list(range(NC))).results
    nc_mid = build_tern_mid()
    in_maps = [{"h_in": np.asarray(r1[c]["h0_d"]), "wt": h1t, "sb_i": h1s}
               for c in range(NC)]
    r2 = run_bass_kernel_spmd(nc_mid, in_maps, core_ids=list(range(NC))).results
    in_maps = [{"h_in": np.asarray(r2[c]["h_out"]), "wt": h2t, "sb_i": h2s}
               for c in range(NC)]
    r3 = run_bass_kernel_spmd(nc_mid, in_maps, core_ids=list(range(NC))).results
    in_maps = [{"h_in": np.asarray(r3[c]["h_out"]), "w3t": w3t, "s3b": s3b,
                "xo": _own_cols(xfm, c), "onc": onc, "onr": onr}
               for c in range(NC)]
    r4 = run_bass_kernel_spmd(build_tern4(), in_maps, core_ids=list(range(NC))).results
    return _scatter_back([np.asarray(r["x4"]) for r in r4])



def emit_halo_ex(nc, ctx, p, PS, d, tag="H"):
    x2 = d["x2"]; ch = d["ch"]; gh = d["gh"]; xh = d["xh"]
    selh = d["selh"]; idf = d["idf"]
    with ExitStack() as ectx:
        sb = lambda n, sh, dt_: ectx.enter_context(nc.sbuf_tensor(n + tag, sh, dt_))
        tails = sb("tails", [128, 3], F32)
        tev = sb("tev", [3, 128], F32)
        ids = sb("ids", [128, 128], F32)
        ghs = sb("ghs", [12, 1024], F32)
        sels = sb("sels", [12, 6], F32)
        hev = sb("hev", [6, 128], F32)
        hx = sb("hx", [128, 6], F32)
        p.op("sync", lambda e: e.dma_start(out=ids[:], in_=idf[:]))
        p.op("sync", lambda e: e.dma_start(out=sels[:], in_=selh[:]))
        for ck, col in [(0, 1021), (1, 2045)]:
            for c in range(8):
                p.op("sync", lambda e, c=c, col=col: e.dma_start(
                    out=tails[:, :], in_=x2[c * 128:(c + 1) * 128, col:col + 3]),
                    deps=("tensor", "scalar", "gpsimd"))
                ld = p.c["sync"]
                p.op("tensor", lambda e: e.transpose(PS[2][0:3, 0:128], tails[:, :],
                                                     ids[:]),
                     deps=[("sync", ld), ("scalar", p.c["scalar"])])
                pe = p.c["tensor"]
                p.op("scalar", lambda e: e.activation(out=tev[:], in_=PS[2][0:3, 0:128],
                                                      func=AF.Copy),
                     deps=[("tensor", pe), ("gpsimd", p.c["gpsimd"])])
                sc = p.c["scalar"]
                p.op("gpsimd", lambda e, ck=ck, c=c: e.dma_start(
                    out=ch[ck * 3:ck * 3 + 3, c * 128:(c + 1) * 128], in_=tev[:]),
                    deps=[("scalar", sc)], dma=True)
        p.op("gpsimd", lambda e: e.collective_compute(
            "AllGather", ALU.bypass,
            replica_groups=[[0, 1], [2, 3], [4, 5], [6, 7]],
            ins=[ch.ap().opt()], outs=[gh.ap().opt()]))
        cc = p.c["gpsimd"]
        p.op("sync", lambda e: e.dma_start(out=ghs[:], in_=gh[:]),
             deps=[("gpsimd", cc)])
        ld = p.c["sync"]
        for c in range(8):
            p.op("tensor", lambda e, c=c: e.matmul(
                PS[2][0:6, 0:128], sels[:], ghs[:, c * 128:(c + 1) * 128],
                start=True, stop=True),
                deps=[("sync", ld), ("scalar", p.c["scalar"]),
                      ("vector", p.c["vector"])])
            pe = p.c["tensor"]
            p.op("scalar", lambda e: e.activation(out=hev[:], in_=PS[2][0:6, 0:128],
                                                  func=AF.Copy),
                 deps=[("tensor", pe), ("tensor", pe)])
            sc = p.c["scalar"]
            p.op("tensor", lambda e: e.transpose(PS[3][:, 0:6], hev[:], ids[0:6, 0:6]),
                 deps=[("scalar", sc), ("vector", p.c["vector"])])
            pe = p.c["tensor"]
            p.op("vector", lambda e: e.tensor_copy(out=hx[:, :], in_=PS[3][:, 0:6]),
                 deps=[("tensor", pe), ("gpsimd", p.c["gpsimd"])])
            vv = p.c["vector"]
            p.op("gpsimd", lambda e, c=c: e.dma_start(
                out=xh[c * 128:(c + 1) * 128, :], in_=hx[:, :]),
                deps=[("vector", vv)], dma=True)


def emit_gef_ex(nc, ctx, p, PS, d, tag="G"):
    ce = d["ce"]; g2 = d["g2"]; gef = d["gef"]; idf = d["idf"]
    with ExitStack() as ectx:
        sb = lambda n, sh, dt_: ectx.enter_context(nc.sbuf_tensor(n + tag, sh, dt_))
        g2s = sb("g2s", [8, 2048], F32)
        ids = sb("ids", [128, 128], F32)
        gv = sb("gv", [128, 8], F32)
        p.op("sync", lambda e: e.dma_start(out=ids[:], in_=idf[:]))
        p.op("gpsimd", lambda e: e.collective_compute(
            "AllGather", ALU.bypass,
            replica_groups=[[0, 1], [2, 3], [4, 5], [6, 7]],
            ins=[ce.ap().opt()], outs=[g2.ap().opt()]),
            deps=("scalar", "vector", "tensor"))
        cc = p.c["gpsimd"]
        p.op("sync", lambda e: e.dma_start(out=g2s[:], in_=g2[:]),
             deps=[("gpsimd", cc)])
        ld = p.c["sync"]
        for oc in range(16):
            p.op("tensor", lambda e, oc=oc: e.transpose(
                PS[2][:, 0:8], g2s[:, oc * 128:(oc + 1) * 128], ids[0:8, 0:8]),
                deps=[("sync", ld), ("vector", p.c["vector"])])
            pe = p.c["tensor"]
            p.op("vector", lambda e: e.tensor_copy(out=gv[:, :], in_=PS[2][:, 0:8]),
                 deps=[("tensor", pe), ("gpsimd", p.c["gpsimd"])])
            vv = p.c["vector"]
            p.op("gpsimd", lambda e, oc=oc: e.dma_start(
                out=gef[:, oc, :], in_=gv[:, :]), deps=[("vector", vv)], dma=True)


def build_fused():
    nc = bass.Bass("TRN2", target_bir_lowering=False)
    di = lambda n, sh, dt_: nc.dram_tensor(n, sh, dt_, kind="ExternalInput")
    xb = di("xb", [D, T], F32)
    xo = di("xo", [D, CTOK], F32)
    wq = di("wq", [D, D], F32); wk = di("wk", [D, D], F32)
    wv = di("wv", [D, D], F32); wp = di("wp", [D, D], F32)
    bp2 = di("bp2", [128, 8], F32)
    onc = di("onc", [128, 1], F32); onr = di("onr", [1, 128], F32)
    mks = di("mks", [4, 8, 128, 512], BF16)
    ali = di("ali", [NH, 128, 128], F32)
    gwt = di("gwt", [D, 8], F32)
    mw1t = di("mw1t", [10, D, 1536], F32)
    mw2t = di("mw2t", [10, 768, D], F32)
    idf = di("idf", [128, 128], F32)
    seld = di("seld", [10, 1280], BF16)
    ipw = di("ipw", [D, 4096], F32); ipb2 = di("ipb2", [128, 32], F32)
    cw2 = di("cw2", [128, 16, 4], F32); cb2 = di("cb2", [128, 16], F32)
    hfl = di("hfl", [128, 2], F32)
    dtw = di("dtw", [2048, 2048], F32); bw = di("bw", [2048, 2048], F32)
    cwt = di("cwt", [2048, 2048], F32)
    dtb2 = di("dtb2", [128, 16], F32); al2 = di("al2", [128, 16], F32)
    selq = di("selq", [128, 8], F32); selh = di("selh", [12, 6], F32)
    ow = di("ow", [2048, D], F32); ob2 = di("ob2", [128, 8], F32)
    d2 = di("d2", [128, 16], F32)
    tw1t = di("tw1t", [D, 4096], F32); tw2t = di("tw2t", [D, 4096], F32)
    s1b = di("s1b", [128, 32], F32); s2b = di("s2b", [128, 32], F32)
    h1t = di("h1t", [4096, 4096], F32); h1s = di("h1s", [128, 32], F32)
    h2t = di("h2t", [4096, 4096], F32); h2s = di("h2s", [128, 32], F32)
    w3t = di("w3t", [4096, D], F32); s3b = di("s3b", [128, 8], F32)
    x4 = nc.dram_tensor("x4", [D, CTOK], F32, kind="ExternalOutput")
    dn = lambda n, sh, dt_: nc.dram_tensor(n, sh, dt_)
    x1 = dn("x1", [D, CTOK], F32)
    x2 = dn("x2", [D, CTOK], F32)
    x3 = dn("x3", [D, CTOK], F32)
    qTd = dn("qTd", [NH, DH, CTOK], BF16)
    kTd = dn("kTd", [NH, DH, T], BF16)
    vd = dn("vd", [NH, T, DH], BF16)
    ctxd = dn("ctxd", [D, CTOK], BF16)
    u_d = dn("u_d", [16, 128, CTOK], BF16)
    sz_d = dn("sz_d", [16, 128, CTOK], BF16)
    dec_d = dn("dec_d", [16, 128, CTOK], F32)
    inc_d = dn("inc_d", [16, 128, CTOK], F32)
    c_d = dn("c_d", [16, 128, CTOK], BF16)
    st_d = dn("st_d", [16, 128, CTOK], F32)
    e_d = dn("e_d", [128, 32], F32)
    f_d = dn("f_d", [128, 32], F32)
    t1_d = dn("ft1_d", [32, 128, CTOK], BF16)
    h0_d = dn("h0_d", [32, 128, CTOK], BF16)
    h1_d = dn("h1_d", [32, 128, CTOK], BF16)
    h2_d = dn("h2_d", [32, 128, CTOK], BF16)
    ch = dn("ch", [6, 1024], F32)
    gh = dn("gh", [12, 1024], F32)
    ce = dn("ce", [4, 2048], F32)
    g2 = dn("g2", [8, 2048], F32)
    xh_d = dn("xh_d", [D, 6], F32)
    gef = dn("gef", [128, 16, 8], F32)

    with ExitStack() as ctx:
        psb = ctx.enter_context(nc.psum_tensor("psb", [128, 4096], F32))
        PS = [psb[:, i * 512:(i + 1) * 512] for i in range(8)] + [psb]
        p = Prog(nc)
        lnw = LNWork(nc, ctx, PS[0], PS[1])
        lnw.load(p, onc, onr)
        emit_attn(nc, ctx, p, lnw, PS, dict(
            xb=xb, xo=xo, wq=wq, wk=wk, wv=wv, wp=wp, bp2=bp2, onc=onc, onr=onr,
            mks=mks, ali=ali, x1=x1, qTd=qTd, kTd=kTd, vd=vd, ctxd=ctxd))
        p.fence()
        emit_moe(nc, ctx, p, lnw, PS, dict(
            xo=x1, gwt=gwt, w1t=mw1t, w2t=mw2t, onc=onc, onr=onr, idf=idf,
            seld=seld, x2=x2))
        p.fence()
        emit_halo_ex(nc, ctx, p, PS, dict(
            x2=x2, ch=ch, gh=gh, xh=xh_d, selh=selh, idf=idf))
        p.fence()
        emit_mpre(nc, ctx, p, lnw, PS, dict(
            xo=x2, xh=xh_d, ipw=ipw, ipb2=ipb2, cw2=cw2, cb2=cb2, hfl=hfl,
            onc=onc, onr=onr, u_d=u_d, sz_d=sz_d))
        p.fence()
        with ExitStack() as mscope:
            mk = emit_mmid(nc, ctx, p, lnw, PS, dict(
                u_d=u_d, dtw=dtw, bw=bw, cw=cwt, dtb2=dtb2, al2=al2, dec_d=dec_d,
                inc_d=inc_d, c_d=c_d, e_d=e_d, ce=ce, idf=idf), ectx=mscope)
            emit_scan(nc, ctx, p, lnw, PS, dict(
                dec_d=dec_d, inc_d=inc_d, onc=onc, st_d=st_d, f_d=f_d, ce=ce,
                idf=idf), ready=mk["inc_done"], ectx=mscope)
        p.fence()
        emit_gef_ex(nc, ctx, p, PS, dict(ce=ce, g2=g2, gef=gef, idf=idf))
        p.fence()
        emit_mpost(nc, ctx, p, lnw, PS, dict(
            st_d=st_d, dec_d=dec_d, c_d=c_d, u_d=u_d, sz_d=sz_d, gef=gef,
            selq=selq, ow=ow, ob2=ob2, d2=d2, xo=x2, onc=onc, onr=onr, x3=x3))
        p.fence()
        emit_tern1(nc, ctx, p, lnw, PS, dict(
            xo=x3, w1t=tw1t, w2t=tw2t, s1b=s1b, s2b=s2b, onc=onc, onr=onr,
            h0_d=h0_d, t1_d=t1_d))
        p.fence()
        emit_tern_mid(nc, ctx, p, lnw, PS, dict(h_in=h0_d, wt=h1t, sb_i=h1s,
                                                h_out=h1_d), tag="U1")
        p.fence()
        emit_tern_mid(nc, ctx, p, lnw, PS, dict(h_in=h1_d, wt=h2t, sb_i=h2s,
                                                h_out=h2_d), tag="U2")
        p.fence()
        emit_tern4(nc, ctx, p, lnw, PS, dict(h_in=h2_d, w3t=w3t, s3b=s3b, xo=x3,
                                             onc=onc, onr=onr, x4=x4))
        p.finalize(ctx)
    return nc


def run_fused(inputs, xfull):
    xfm = _fm(xfull)
    onc = np.ones((128, 1), dtype=np.float32)
    onr = np.ones((1, 128), dtype=np.float32)
    idf = np.eye(128, dtype=np.float32)
    seld = np.zeros((10, 1280), dtype=BF)
    for ex in range(10):
        seld[ex, ex * 128:(ex + 1) * 128] = 1.0
    gwt, mw1t, mw2t = moe_weights(inputs)
    ipw = np.ascontiguousarray(np.asarray(inputs["L1_attn_in_proj_w"], np.float32).T)
    ipb2 = chunkcol(inputs["L1_attn_in_proj_b"])
    cw2 = np.ascontiguousarray(np.asarray(inputs["L1_attn_conv_w"], np.float32)[:, 0, :]
                               .reshape(16, 128, 4).transpose(1, 0, 2))
    cb2 = chunkcol(inputs["L1_attn_conv_b"])
    dtw = np.ascontiguousarray(np.asarray(inputs["L1_attn_dt_w"], np.float32).T)
    bw = np.ascontiguousarray(np.asarray(inputs["L1_attn_b_w"], np.float32).T)
    cwt = np.ascontiguousarray(np.asarray(inputs["L1_attn_c_w"], np.float32).T)
    dtb2 = chunkcol(inputs["L1_attn_dt_b"])
    al2 = chunkcol(inputs["L1_attn_A_log"])
    d2 = chunkcol(inputs["L1_attn_D_param"])
    ow = np.ascontiguousarray(np.asarray(inputs["L1_attn_out_w"], np.float32).T)
    ob2 = chunkcol(inputs["L1_attn_out_b"])
    tw1t = np.ascontiguousarray(np.asarray(inputs["L1_ffn_w1_w"], np.float32).T)
    tw2t = np.ascontiguousarray(np.asarray(inputs["L1_ffn_w2_w"], np.float32).T)
    h1t = np.ascontiguousarray(np.asarray(inputs["L1_ffn_h1_w"], np.float32).T)
    h2t = np.ascontiguousarray(np.asarray(inputs["L1_ffn_h2_w"], np.float32).T)
    w3t = np.ascontiguousarray(np.asarray(inputs["L1_ffn_w3_w"], np.float32).T)
    s1b = chunkcol(inputs["L1_ffn_w1_s"]); s2b = chunkcol(inputs["L1_ffn_w2_s"])
    h1s = chunkcol(inputs["L1_ffn_h1_s"]); h2s = chunkcol(inputs["L1_ffn_h2_s"])
    s3b = chunkcol(inputs["L1_ffn_w3_s"])
    wq = np.asarray(inputs["L0_attn_W_query"], dtype=np.float32)
    wk = np.asarray(inputs["L0_attn_W_key"], dtype=np.float32)
    wv = np.asarray(inputs["L0_attn_W_value"], dtype=np.float32)
    wp = np.asarray(inputs["L0_attn_W_proj"], dtype=np.float32)
    bp2 = chunkcol(inputs["L0_attn_b_proj"])

    in_maps = []
    for c in range(NC):
        b, qr = core_quarters(c)
        cst = attn_consts(core_qstarts(c))
        hfl = np.zeros((128, 2), dtype=np.float32)
        selh = np.zeros((12, 6), dtype=np.float32)
        for ck, (a, bb) in enumerate(qr):
            if a > 0:
                hfl[:, ck] = 1.0
        if c % 2 == 0:
            for j in range(3):
                selh[0 + j, 0 + j] = 1.0
                selh[9 + j, 3 + j] = 1.0
        else:
            for j in range(3):
                selh[0 + j, 0 + j] = 1.0
                selh[6 + j, 3 + j] = 1.0
        sq = np.zeros((128, 8), dtype=np.float32)
        quarters = (0, 3) if c % 2 == 0 else (1, 2)
        sq[:, 0 * 4 + quarters[0]] = 1.0
        sq[:, 1 * 4 + quarters[1]] = 1.0
        in_maps.append({
            "xb": xfm[b], "xo": _own_cols(xfm, c),
            "wq": wq, "wk": wk, "wv": wv, "wp": wp, "bp2": bp2,
            "onc": onc, "onr": onr, "mks": cst["mks"], "ali": cst["ali"],
            "gwt": gwt, "mw1t": mw1t, "mw2t": mw2t, "idf": idf, "seld": seld,
            "ipw": ipw, "ipb2": ipb2, "cw2": cw2, "cb2": cb2, "hfl": hfl,
            "dtw": dtw, "bw": bw, "cwt": cwt, "dtb2": dtb2, "al2": al2,
            "selq": sq, "selh": selh, "ow": ow, "ob2": ob2, "d2": d2,
            "tw1t": tw1t, "tw2t": tw2t, "s1b": s1b, "s2b": s2b,
            "h1t": h1t, "h1s": h1s, "h2t": h2t, "h2s": h2s,
            "w3t": w3t, "s3b": s3b,
        })
    nc = build_fused()
    res = run_bass_kernel_spmd(nc, in_maps, core_ids=list(range(NC)))
    return _scatter_back([np.asarray(r["x4"]) for r in res.results])
